# version 08
# baseline (speedup 1.0000x reference)
import numpy as np
import ml_dtypes
from contextlib import ExitStack

import concourse.bass as bass
import concourse.mybir as mybir
from concourse.bass_utils import run_bass_kernel_spmd

F32 = mybir.dt.float32
BF16 = mybir.dt.bfloat16
U32 = mybir.dt.uint32
U16 = mybir.dt.uint16
U8 = mybir.dt.uint8
F16 = mybir.dt.float16
AF = mybir.ActivationFunctionType
ALU = mybir.AluOpType
AX = mybir.AxisListType.X
BF16NP = ml_dtypes.bfloat16

KG = 512
EPS_RMS = 1e-6
EPS_LN = 1e-5


def cfg_full():
    return dict(B=4, S=2048, H=4096, NH=32, DN=128, DR=64, DV=128,
                QL=1536, KL=512, FI=11008, HPG=4)


def cfg_small():
    return dict(B=4, S=1024, H=1024, NH=4, DN=128, DR=64, DV=128,
                QL=512, KL=512, FI=2048, HPG=4)


def _derived(c):
    d = dict(c)
    d["nKT"] = c["H"] // 128
    d["nQT"] = c["S"] // 128
    d["TPC"] = c["S"] // 2
    d["nJT"] = d["TPC"] // 128
    d["nKpack"] = c["H"] // 1024
    d["nG"] = c["H"] // 128
    d["nGd"] = c["FI"] // 128
    d["packd"] = c["FI"] // 8
    d["nPd"] = (d["packd"] + 127) // 128
    d["nQLT"] = c["QL"] // 128
    d["nKLT"] = c["KL"] // 128
    d["nKVa"] = (c["KL"] + c["DR"] + 127) // 128
    d["nHG"] = c["NH"] // c["HPG"]
    d["nKC"] = c["S"] // KG
    d["nTG"] = d["TPC"] // KG
    d["nNT"] = d["TPC"] // 512
    d["nKo"] = c["NH"] * c["DV"] // 128
    assert c["H"] % 1024 == 0 and d["TPC"] % 512 == 0
    return d


def _qtiles_for_parity(d, parity):
    picks = (0, 1, 6, 7) if parity == 0 else (2, 3, 4, 5)
    return [g for g in range(d["nQT"]) if g % 8 in picks]


def _counts(d, qt):
    return [g // 4 + 1 for g in qt]


def _dwidth(d, P):
    return min(128, d["packd"] - 128 * P)


def _gu_positions(d):
    pos, acc = [], 0
    for T in range(8 * d["nPd"]):
        pos.append(acc)
        acc += _dwidth(d, T // 8)
    pos.append(acc)
    assert acc == d["FI"]
    return pos


def _sigma_h(d):
    H = d["H"]
    sig = np.empty(H, dtype=np.int64)
    for kp in range(d["nKpack"]):
        for i in range(8):
            for p in range(128):
                sig[1024 * kp + 128 * i + p] = 1024 * kp + 8 * p + i
    return sig


def _gate_perm(d):
    pos = _gu_positions(d)
    perm = np.empty(d["FI"], dtype=np.int64)
    for T in range(8 * d["nPd"]):
        w = _dwidth(d, T // 8)
        P, i = T // 8, T % 8
        for cc in range(w):
            perm[pos[T] + cc] = 1024 * P + 8 * cc + i
    return perm


def _np_bf16(x):
    return np.ascontiguousarray(np.asarray(x, dtype=np.float32).astype(BF16NP))


def _np_f16(x):
    return np.ascontiguousarray(np.asarray(x, dtype=np.float32).astype(np.float16))


def _consts(d):
    out = {}
    out["identb"] = _np_bf16(np.eye(128))
    out["identh"] = _np_f16(np.eye(128))
    out["identf"] = np.ascontiguousarray(np.eye(128, dtype=np.float32))
    out["onescb"] = _np_f16(np.ones((128, 1)))
    out["ones1f"] = np.ones((1, 128), dtype=np.float32)
    sw = np.zeros((d["DR"], d["DR"]), dtype=np.float32)
    for k in range(d["DR"]):
        sw[k, k ^ 1] = 1.0
    out["swapb"] = _np_f16(sw)
    eg = np.zeros((d["nKpack"], d["nG"], 128), dtype=np.float32)
    for kp in range(d["nKpack"]):
        for p in range(128):
            eg[kp, 8 * kp + p // 16, p] = 1.0
    out["egb"] = _np_f16(eg.reshape(d["nKpack"] * d["nG"], 128))
    b32 = np.zeros((d["nKpack"], 128, d["nG"]), dtype=np.float32)
    for kp in range(d["nKpack"]):
        for p in range(128):
            b32[kp, p, 8 * kp + p // 16] = 1.0
    out["b32b"] = _np_f16(b32.reshape(d["nKpack"] * 128, d["nG"]))
    b86 = np.zeros((d["nPd"], 128, d["nGd"]), dtype=np.float32)
    for P in range(d["nPd"]):
        for p in range(_dwidth(d, P)):
            b86[P, p, 8 * P + p // 16] = 1.0
    out["b86b"] = _np_f16(b86.reshape(d["nPd"] * 128, d["nGd"]))
    ed = np.zeros((d["nPd"], d["nGd"], 128), dtype=np.float32)
    for P in range(d["nPd"]):
        for p in range(_dwidth(d, P)):
            ed[P, 8 * P + p // 16, p] = 1.0
    out["edb"] = _np_f16(ed.reshape(d["nPd"] * d["nGd"], 128))
    return out


def _rope_tables(d, positions):
    DR = d["DR"]
    freqs = 1.0 / 10000.0 ** (np.arange(0, DR, 2, dtype=np.float64) / DR)
    ang = np.outer(np.asarray(positions, dtype=np.float64), freqs)
    cos = np.cos(ang).astype(np.float32).T
    sin = np.sin(ang).astype(np.float32).T
    cose = np.repeat(cos, 2, axis=0)
    sine2 = np.empty_like(cose)
    sine2[0::2] = -sin
    sine2[1::2] = sin
    return np.ascontiguousarray(cose), np.ascontiguousarray(sine2)


def _shared_weight_views(d, W):
    sig = _sigma_h(d)
    perm = _gate_perm(d)
    FI = d["FI"]
    v = {}
    v["wqaT"] = np.ascontiguousarray(np.asarray(W["wq_a"]).T[sig])
    v["wkvaT"] = np.ascontiguousarray(np.asarray(W["wkv_a"]).T[sig])
    v["wqbT"] = np.ascontiguousarray(np.asarray(W["wq_b"]).T)
    v["wkvbT"] = np.ascontiguousarray(np.asarray(W["wkv_b"]).T)
    v["woT"] = np.ascontiguousarray(np.asarray(W["wo"]).T)
    qg = np.asarray(W["qweight_gu"])
    sg = np.asarray(W["scales_gu"])
    v["qwg_g"] = np.ascontiguousarray(qg[:, perm])
    v["qwg_u"] = np.ascontiguousarray(qg[:, FI + perm])
    v["scg_g"] = np.ascontiguousarray(sg[:, perm])
    v["scg_u"] = np.ascontiguousarray(sg[:, FI + perm])
    v["scg_nat"] = np.ascontiguousarray(sg)
    v["qzg"] = np.ascontiguousarray(np.asarray(W["qzeros_gu"]))
    v["qwd"] = np.ascontiguousarray(np.asarray(W["qweight_down"]))
    v["scd"] = np.ascontiguousarray(np.asarray(W["scales_down"]))
    v["qzd"] = np.ascontiguousarray(np.asarray(W["qzeros_down"]))
    v["qnw"] = np.ascontiguousarray(
        np.asarray(W["q_norm_w"]).reshape(d["nQLT"], 128).T)
    v["kvnw"] = np.ascontiguousarray(
        np.asarray(W["kv_norm_w"]).reshape(d["nKLT"], 128).T)
    return v


def core_inputs(d, W, core_id, shared=None, consts=None):
    if shared is None:
        shared = _shared_weight_views(d, W)
    if consts is None:
        consts = _consts(d)
    b, par = core_id // 2, core_id % 2
    qt = _qtiles_for_parity(d, par)
    cnt = _counts(d, qt)
    own_rows = np.concatenate([np.arange(g * 128, (g + 1) * 128) for g in qt])
    x_all = np.ascontiguousarray(np.asarray(W["x"])[b], dtype=np.float32)
    x_own = np.ascontiguousarray(x_all[own_rows])
    masks = np.zeros((d["nJT"], 128, KG), dtype=BF16NP)
    for j, g in enumerate(qt):
        delta = 128 * g - KG * (cnt[j] - 1)
        f = np.arange(KG)[None, :]
        pp = np.arange(128)[:, None]
        masks[j] = (f <= pp + delta).astype(BF16NP)
    cose_k, sine2_k = _rope_tables(d, np.arange(d["S"]))
    cose_q, sine2_q = _rope_tables(d, own_rows)
    m = dict(x_all=x_all, x_own=x_own, masks=np.ascontiguousarray(masks),
             cose_k=cose_k, sine2_k=sine2_k,
             cose_q=cose_q.astype(np.float16),
             sine2_q=sine2_q.astype(np.float16))
    m.update(shared)
    m.update(consts)
    return m



ENGS = ("tensor", "vector", "scalar", "gpsimd", "sync")


class Buf:
    __slots__ = ("name", "w", "r", "dsem", "dcount", "dqueue")

    def __init__(self, name=""):
        self.name = name
        self.w = {}
        self.r = {}
        self.dsem = None
        self.dcount = 0
        self.dqueue = None


class _Op:
    __slots__ = ("fns", "waits", "inc_sem", "inc_amt", "tag")

    def __init__(self, fns, waits, inc_sem, inc_amt, tag=""):
        self.fns = fns
        self.waits = waits
        self.inc_sem = inc_sem
        self.inc_amt = inc_amt
        self.tag = tag


class Prog:
    def __init__(self, nc):
        self.nc = nc
        self.ops = {e: [] for e in ENGS}
        self.count = {e: 0 for e in ENGS}
        self.water = {e: {} for e in ENGS}
        self.pending = {e: {} for e in ENGS}
        self.dsems = {}

    @staticmethod
    def _merge(dst, src):
        for k, v in src.items():
            if dst.get(k, -1) < v:
                dst[k] = v

    def emit(self, eng, fn=None, reads=(), writes=(), dma=False, fns=None,
             tag=""):
        import inspect
        if not tag:
            fr = inspect.currentframe().f_back
            if fr.f_code.co_name in ("pe", "act", "dve", "dma"):
                fr = fr.f_back
            tag = f"{fr.f_code.co_name}:{fr.f_lineno}"
        want = {}
        self._merge(want, self.pending[eng])
        self.pending[eng] = {}
        for bb in reads:
            self._merge(want, bb.w)
        for bb in writes:
            self._merge(want, bb.w)
            self._merge(want, bb.r)
        waits = []
        wm = self.water[eng]
        for sname, val in sorted(want.items()):
            if eng == "tensor" and sname == "tensor":
                continue
            if wm.get(sname, -1) >= val:
                continue
            wm[sname] = val
            waits.append((sname, val))
        if dma:
            assert writes, "dma op must declare a written buf"
            tb = writes[0]
            if tb.dsem is None:
                tb.dsem = "ds_" + tb.name
                tb.dqueue = eng
                assert tb.dsem not in self.dsems, tb.dsem
                self.dsems[tb.dsem] = 0
            assert tb.dqueue == eng, \
                f"buf {tb.name} dma'd from both {tb.dqueue} and {eng}"
            self.dsems[tb.dsem] += 16
            tok = {tb.dsem: self.dsems[tb.dsem]}
            self.ops[eng].append(_Op(fns if fns is not None else [fn], waits,
                                     tb.dsem, 16, tag))
            for bb in reads:
                self._merge(bb.r, tok)
            for bb in writes:
                bb.w = dict(tok)
                bb.r = {}
            return tok
        amt = 1
        self.count[eng] += amt
        tok = {eng: self.count[eng]}
        self.ops[eng].append(_Op(fns if fns is not None else [fn], waits,
                                 eng, amt, tag))
        for bb in reads:
            self._merge(bb.r, tok)
        for bb in writes:
            bb.w = dict(tok)
            bb.r = {}
        return tok

    def fence(self):
        snap = {e: cnt for e, cnt in self.count.items() if cnt > 0}
        snap.update({s: cnt for s, cnt in self.dsems.items() if cnt > 0})
        for e in ENGS:
            self._merge(self.pending[e], snap)

    def n_instructions(self):
        return sum(len(v) + sum(len(o.fns) - 1 for o in v)
                   for v in self.ops.values())

    def realize(self, es):
        handles = {e: es.enter_context(self.nc.semaphore("sem_" + e))
                   for e in ENGS}
        for s in self.dsems:
            handles[s] = es.enter_context(self.nc.semaphore(s))
        block = es.enter_context(self.nc.Block())
        prog = self

        def mk(ename):
            def body(eng):
                for op in prog.ops[ename]:
                    for sname, val in op.waits:
                        eng.wait_ge(handles[sname], val)
                    inst = None
                    for f in op.fns:
                        inst = f(eng)
                    if inst is not None:
                        inst.then_inc(handles[op.inc_sem], op.inc_amt)
            return body

        block.tensor(mk("tensor"))
        block.vector(mk("vector"))
        block.scalar(mk("scalar"))
        block.gpsimd(mk("gpsimd"))
        block.sync(mk("sync"))



def build_core_program(c, taps=(), dbg=()):
    d = _derived(c)
    H, S, NH, DN, DR, DV = c["H"], c["S"], c["NH"], c["DN"], c["DR"], c["DV"]
    QL, KL, FI, HPG = c["QL"], c["KL"], c["FI"], c["HPG"]
    nKT, nJT, TPC, nKC = d["nKT"], d["nJT"], d["TPC"], d["nKC"]
    nQLT, nKLT, nKVa = d["nQLT"], d["nKLT"], d["nKVa"]
    nKpack, nG, nGd, nPd = d["nKpack"], d["nG"], d["nGd"], d["nPd"]
    nHG, nNT, nTG, nKo = d["nHG"], d["nNT"], d["nTG"], d["nKo"]
    DH = DN + DR
    DKV = DN + DV
    cnt = _counts(d, _qtiles_for_parity(d, 0))
    scale = 1.0 / float(np.sqrt(DH))
    gupos = _gu_positions(d)

    nc = bass.Bass("TRN2", target_bir_lowering=False)

    def din(name, shape, dt):
        return nc.dram_tensor(name, list(shape), dt, kind="ExternalInput").ap()

    x_all = din("x_all", (S, H), F32)
    x_own = din("x_own", (TPC, H), F32)
    wqaT = din("wqaT", (H, QL), F32)
    wkvaT = din("wkvaT", (H, KL + DR), F32)
    wqbT = din("wqbT", (QL, NH * DH), F32)
    wkvbT = din("wkvbT", (KL, NH * DKV), F32)
    woT = din("woT", (NH * DV, H), F32)
    qwg_g = din("qwg_g", (H // 8, FI), U32)
    qwg_u = din("qwg_u", (H // 8, FI), U32)
    scg_g = din("scg_g", (nG, FI), F32)
    scg_u = din("scg_u", (nG, FI), F32)
    scg_nat = din("scg_nat", (nG, 2 * FI), F32)
    qzg = din("qzg", (nG, 2 * FI // 8), U32)
    qwd = din("qwd", (d["packd"], H), U32)
    scd = din("scd", (nGd, H), F32)
    qzd = din("qzd", (nGd, H // 8), U32)
    qnw = din("qnw", (128, nQLT), F32)
    kvnw = din("kvnw", (128, nKLT), F32)
    masks = din("masks", (nJT, 128, KG), BF16)
    cose_k = din("cose_k", (DR, S), F32)
    sine2_k = din("sine2_k", (DR, S), F32)
    cose_q = din("cose_q", (DR, TPC), F16)
    sine2_q = din("sine2_q", (DR, TPC), F16)
    identb = din("identb", (128, 128), BF16)
    identh = din("identh", (128, 128), F16)
    identf = din("identf", (128, 128), F32)
    onescb = din("onescb", (128, 1), F16)
    ones1f = din("ones1f", (1, 128), F32)
    swapb = din("swapb", (DR, DR), F16)
    egb = din("egb", (nKpack * nG, 128), F16)
    b32b = din("b32b", (nKpack * 128, nG), F16)
    b86b = din("b86b", (nPd * 128, nGd), F16)
    edb = din("edb", (nPd * nGd, 128), F16)

    out_own = nc.dram_tensor("out_own", [TPC, H], F32,
                             kind="ExternalOutput").ap()
    xnT_scr = nc.dram_tensor("xnT_scr", [H, TPC], F16, kind="Internal").ap()
    h_scr = nc.dram_tensor("h_scr", [8 * nPd * 128, TPC], F16,
                           kind="Internal").ap()
    wo_scr = nc.dram_tensor("wo_scr", [TPC, H], F16, kind="Internal").ap()
    attn_scr = nc.dram_tensor("attn_scr", [NH * 128, TPC], F16,
                              kind="Internal").ap()
    tap_aps = {}

    def tapout(name, shape, dt):
        if name in taps:
            tap_aps[name] = nc.dram_tensor("tap_" + name, list(shape), dt,
                                           kind="ExternalOutput").ap()
            return True
        return False

    p = Prog(nc)
    out_B = Buf("out")
    xnT_scr_B = Buf("xnT_scr")
    h_scr_B = Buf("h_scr")
    wo_scr_B = Buf("wo_scr")
    attn_scr_B = Buf("attn_scr")

    def sb(stack, name, shape, dt):
        t = stack.enter_context(nc.sbuf_tensor(name, list(shape), dt))
        return t, Buf(name)

    def pe(fns, reads, writes):
        return p.emit("tensor", None, fns=fns, reads=tuple(reads),
                      writes=tuple(writes))

    def act(fn, reads, writes):
        return p.emit("scalar", fn, reads=tuple(reads), writes=tuple(writes))

    def dve(fn, reads, writes):
        return p.emit("vector", fn, reads=tuple(reads), writes=tuple(writes))

    def dma(q, fn, reads, writes):
        return p.emit(q, fn, reads=tuple(reads), writes=tuple(writes), dma=True)

    es = ExitStack()
    with es:
        pb, pbB = [], []
        for i in range(8):
            t = es.enter_context(nc.psum_tensor(f"pb{i}", [128, 512], F32))
            pb.append(t)
            pbB.append(Buf(f"pb{i}"))

        identb_s, identb_B = sb(es, "identb_s", (128, 128), BF16)
        identh_s, identh_B = sb(es, "identh_s", (128, 128), F16)
        identf_s, identf_B = sb(es, "identf_s", (128, 128), F32)
        onescb_s, onescb_B = sb(es, "onescb_s", (128, 1), F16)
        ones1f_s, ones1f_B = sb(es, "ones1f_s", (1, 128), F32)
        swap_s, swap_B = sb(es, "swap_s", (DR, DR), F16)
        qnw_s, qnw_B = sb(es, "qnw_s", (128, nQLT), F32)
        kvnw_s, kvnw_B = sb(es, "kvnw_s", (128, nKLT), F32)
        eg_s, eg_B = sb(es, "eg_s", (nG, nKpack * 128), F16)
        b32_s, b32_B = sb(es, "b32_s", (128, nKpack * nG), F16)
        b86_s, b86_B = sb(es, "b86_s", (128, nPd * nGd), F16)
        ed_s, ed_B = sb(es, "ed_s", (nGd, nPd * 128), F16)
        hs_s, hs_B = sb(es, "hs", (nGd, nTG * 512), F16)
        xs_s, xs_B = sb(es, "xs", (nG, nTG * 512), F16)

        for t_, bb_, src_ in (
                (identb_s, identb_B, identb), (identh_s, identh_B, identh),
                (identf_s, identf_B, identf),
                (onescb_s, onescb_B, onescb), (ones1f_s, ones1f_B, ones1f),
                (swap_s, swap_B, swapb), (qnw_s, qnw_B, qnw),
                (kvnw_s, kvnw_B, kvnw)):
            dma("sync", (lambda e, t_=t_, src_=src_: e.dma_start(
                out=t_[:], in_=src_)), (), (bb_,))
        dma("sync", lambda e: e.dma_start(
            out=eg_s[:].rearrange("g (kp q) -> g kp q", kp=nKpack),
            in_=egb.rearrange("(kp g) q -> g kp q", g=nG)), (), (eg_B,))
        dma("sync", lambda e: e.dma_start(
            out=b32_s[:].rearrange("q (kp g) -> q kp g", kp=nKpack),
            in_=b32b.rearrange("(kp q) g -> q kp g", q=128)), (), (b32_B,))
        dma("sync", lambda e: e.dma_start(
            out=b86_s[:].rearrange("q (pp g) -> q pp g", pp=nPd),
            in_=b86b.rearrange("(pp q) g -> q pp g", q=128)), (), (b86_B,))
        dma("sync", lambda e: e.dma_start(
            out=ed_s[:].rearrange("g (pp q) -> g pp q", pp=nPd),
            in_=edb.rearrange("(pp g) q -> g pp q", g=nGd)), (), (ed_B,))

        def emit_ln(src_ap, xb, xbB, xnb, xnbB, stat_s, stat_B):
            dma("sync", (lambda e, src_ap=src_ap, xb=xb:
                         e.dma_start(out=xb[:], in_=src_ap)), (), (xbB,))
            dve((lambda e, xb=xb: e.tensor_reduce(
                stat_s[:, 0:1], xb[:], AX, ALU.add)), (xbB,), (stat_B,))
            act((lambda e, xb=xb, xnb=xnb: e.activation(
                out=xnb[:].bitcast(F16), in_=xb[:], func=AF.Square,
                accum_out=stat_s[:, 1:2])), (xbB,), (xnbB, stat_B))
            dve(lambda e: e.tensor_scalar(
                out=stat_s[:, 2:3], in0=stat_s[:, 0:1], scalar1=1.0 / H,
                scalar2=None, op0=ALU.mult), (stat_B,), (stat_B,))
            dve(lambda e: e.tensor_tensor(
                out=stat_s[:, 3:4], in0=stat_s[:, 2:3], in1=stat_s[:, 2:3],
                op=ALU.mult), (stat_B,), (stat_B,))
            dve(lambda e: e.tensor_scalar(
                out=stat_s[:, 4:5], in0=stat_s[:, 1:2], scalar1=1.0 / H,
                scalar2=EPS_LN, op0=ALU.mult, op1=ALU.add), (stat_B,), (stat_B,))
            dve(lambda e: e.tensor_tensor(
                out=stat_s[:, 4:5], in0=stat_s[:, 4:5], in1=stat_s[:, 3:4],
                op=ALU.subtract), (stat_B,), (stat_B,))
            act(lambda e: e.activation(
                out=stat_s[:, 5:6], in_=stat_s[:, 4:5], func=AF.Ln),
                (stat_B,), (stat_B,))
            act(lambda e: e.activation(
                out=stat_s[:, 5:6], in_=stat_s[:, 5:6], func=AF.Exp,
                scale=-0.5), (stat_B,), (stat_B,))
            dve(lambda e: e.scalar_tensor_tensor(
                out=stat_s[:, 6:7], in0=stat_s[:, 2:3], scalar=-1.0,
                in1=stat_s[:, 5:6], op0=ALU.mult, op1=ALU.mult),
                (stat_B,), (stat_B,))
            dve((lambda e, xb=xb, xnb=xnb: e.scalar_tensor_tensor(
                out=xnb[:], in0=xb[:], scalar=stat_s[:, 5:6],
                in1=stat_s[:, 6:7].broadcast_to([128, H]),
                op0=ALU.mult, op1=ALU.add)),
                (xbB, stat_B), (xnbB,))

        def emit_xpose(xnb, xnbB, dest_fn, dest_B):
            for a in range(nKT // 8):
                bank, bankB = pb[4 + (a % 2)], pbB[4 + (a % 2)]
                fns = []
                for i8 in range(8):
                    kt = 8 * a + i8
                    kp, i = kt // 8, kt % 8
                    src = xnb[:, 1024 * kp + i: 1024 * (kp + 1): 8]
                    dst = bank[:].bitcast(F16)[:, i8 * 128:(i8 + 1) * 128]
                    fns.append((lambda e, src=src, dst=dst:
                                e.transpose(dst, src, identh_s[:])))
                pe(fns, [xnbB, identh_B], [bankB])
                act((lambda e, bank=bank, dst_ap=dest_fn(a):
                     e.activation(out=dst_ap,
                                  in_=bank[:].bitcast(F16)[:, 0:1024]
                                  .rearrange("q (j f) -> q j f", j=8),
                                  func=AF.Copy)), [bankB], [dest_B])

        s1 = ExitStack()
        with s1:
            kvcn_s, kvcn_B = sb(s1, "kvcn", (128, nKLT * S), F16)
            kpe_s, kpe_B = sb(s1, "kpe", (DR, S), F16)
            kpesq_s, kpesq_B = sb(s1, "kpesq", (1, S), F16)
            masks_s, masks_B = sb(s1, "masks_s", (128, nJT * KG), BF16)
            sqbuf_s, sqbuf_B = sb(s1, "sqbuf", (128, 512), F16)
            stat2 = [sb(s1, f"stat{i}", (128, 8), F32) for i in range(2)]
            coseq_s, coseq_B = sb(s1, "coseq_s", (DR, TPC), F16)
            sineq_s, sineq_B = sb(s1, "sineq_s", (DR, TPC), F16)
            dma("sync", lambda e: e.dma_start(
                out=masks_s[:].rearrange("q (j f) -> q j f", j=nJT),
                in_=masks.rearrange("j q f -> q j f")), (), (masks_B,))
            dma("sync", lambda e: e.dma_start(out=coseq_s[:], in_=cose_q),
                (), (coseq_B,))
            dma("sync", lambda e: e.dma_start(out=sineq_s[:], in_=sine2_q),
                (), (sineq_B,))

            s2a = ExitStack()
            if True:
                with s2a:
                    cosek_s, cosek_B = sb(s2a, "cosek_s", (DR, S), F32)
                    sinek_s, sinek_B = sb(s2a, "sinek_s", (DR, S), F32)
                    dma("sync", lambda e: e.dma_start(out=cosek_s[:],
                                                      in_=cose_k),
                        (), (cosek_B,))
                    dma("sync", lambda e: e.dma_start(out=sinek_s[:],
                                                      in_=sine2_k),
                        (), (sinek_B,))
                    xbufs = [sb(s2a, f"xbuf{i}", (128, H), F32)
                             for i in range(1)]
                    xnbufs = [sb(s2a, f"xnbuf{i}", (128, H), F16)
                              for i in range(2)]
                    xstg_s, xstg_B = sb(s2a, "xstg", (128, nKT * 128), F16)

                    xsgv = xstg_s[:].rearrange("q (kt t) -> q kt t", kt=nKT)
                    for j in range(nJT):
                        xb, xbB = xbufs[0]
                        xnb, xnbB = xnbufs[j % 2]
                        st_, stB_ = stat2[j % 2]
                        emit_ln(x_own[j * 128:(j + 1) * 128, :],
                                xb, xbB, xnb, xnbB, st_, stB_)
                        emit_xpose(xnb, xnbB,
                                   (lambda a: xsgv[:, 8 * a: 8 * a + 8, :]),
                                   xstg_B)
                        dma("sync", (lambda e, j=j: e.dma_start(
                            out=xnT_scr.rearrange("(kt q) t -> q kt t",
                                                  q=128)[
                                :, :, j * 128:(j + 1) * 128],
                            in_=xsgv)), (xstg_B,), (xnT_scr_B,))

                    wkva_s, wkva_B = sb(s2a, "wkva", (128, nKT * (KL + DR)),
                                        F16)
                    dma("gpsimd", lambda e: e.dma_start(
                        out=wkva_s[:].rearrange("q (kt m) -> q kt m", kt=nKT),
                        in_=wkvaT.rearrange("(kt q) m -> q kt m", q=128)),
                        (), (wkva_B,))
                    xnTch_s, xnTch_B = sb(s2a, "xnTch", (128, nKT * KG), F16)
                    kvstat_s, kvstat_B = sb(s2a, "kvstat", (1, KG), F32)
                    kvrb_s, kvrb_B = sb(s2a, "kvrb", (128, KG), F32)
                    kpet_s, kpet_B = sb(s2a, "kpet", (DR, 3 * KG), F32)
                    kpebf_s, kpebf_B = sb(s2a, "kpebf", (DR, KG), F16)
                    xchv = xnTch_s[:].rearrange("q (kt t) -> q kt t", kt=nKT)

                    for ch in range(nKC):
                        t0 = ch * KG
                        for tt in range(KG // 128):
                            xb, xbB = xbufs[0]
                            xnb, xnbB = xnbufs[tt % 2]
                            st_, stB_ = stat2[tt % 2]
                            emit_ln(x_all[t0 + tt * 128:
                                          t0 + (tt + 1) * 128, :],
                                    xb, xbB, xnb, xnbB, st_, stB_)
                            emit_xpose(xnb, xnbB,
                                       (lambda a, tt=tt: xchv[
                                           :, 8 * a: 8 * a + 8,
                                           tt * 128:(tt + 1) * 128]),
                                       xnTch_B)
                        for mt in range(nKVa):
                            mw = min(128, KL + DR - mt * 128)
                            bank, bankB = pb[mt % 2], pbB[mt % 2]
                            fns = []
                            for kt in range(nKT):
                                lhsT = wkva_s[:, kt * (KL + DR) + mt * 128:
                                              kt * (KL + DR) + mt * 128 + mw]
                                rhs = xnTch_s[:, kt * KG:(kt + 1) * KG]
                                fns.append((lambda e, lhsT=lhsT, rhs=rhs,
                                            bank=bank, mw=mw, first=(kt == 0),
                                            last=(kt == nKT - 1):
                                            e.matmul(bank[0:mw, :], lhsT, rhs,
                                                     start=first, stop=last)))
                            pe(fns, [wkva_B, xnTch_B], [bankB])
                            if mt < nKLT:
                                act((lambda e, bank=bank, mt=mt, t0=t0:
                                     e.activation(
                                         out=kvcn_s[:, mt * S + t0:
                                                    mt * S + t0 + KG],
                                         in_=bank[:], func=AF.Copy)),
                                    [bankB], [kvcn_B])
                                act((lambda e, bank=bank: e.activation(
                                    out=sqbuf_s[:, 0:KG], in_=bank[:],
                                    func=AF.Square)), [bankB], [sqbuf_B])
                                pe([(lambda e, mt=mt: e.matmul(
                                    pb[2][0:1, :], onescb_s[:],
                                    sqbuf_s[:, 0:KG],
                                    start=(mt == 0), stop=(mt == nKLT - 1)))],
                                    [sqbuf_B, onescb_B], [pbB[2]])
                            else:
                                act((lambda e, bank=bank, mw=mw: e.activation(
                                    out=kpet_s[0:mw, 0:KG], in_=bank[0:mw, :],
                                    func=AF.Copy)), [bankB], [kpet_B])
                        dve(lambda e: e.tensor_scalar(
                            out=kvstat_s[:], in0=pb[2][0:1, :],
                            scalar1=1.0 / KL, scalar2=EPS_RMS,
                            op0=ALU.mult, op1=ALU.add), (pbB[2],), (kvstat_B,))
                        act(lambda e: e.activation(
                            out=kvstat_s[:], in_=kvstat_s[:], func=AF.Ln),
                            (kvstat_B,), (kvstat_B,))
                        act(lambda e: e.activation(
                            out=kvstat_s[:], in_=kvstat_s[:], func=AF.Exp,
                            scale=-0.5), (kvstat_B,), (kvstat_B,))
                        pe([lambda e: e.matmul(pb[3][:], ones1f_s[:],
                                               kvstat_s[:], start=True,
                                               stop=True)],
                           [ones1f_B, kvstat_B], [pbB[3]])
                        dve(lambda e: e.tensor_copy(out=kvrb_s[:],
                                                    in_=pb[3][:]),
                            (pbB[3],), (kvrb_B,))
                        for mt in range(nKLT):
                            dve((lambda e, mt=mt, t0=t0:
                                 e.scalar_tensor_tensor(
                                     out=kvcn_s[:, mt * S + t0:
                                                mt * S + t0 + KG],
                                     in0=kvcn_s[:, mt * S + t0:
                                                mt * S + t0 + KG],
                                     scalar=kvnw_s[:, mt: mt + 1],
                                     in1=kvrb_s[:],
                                     op0=ALU.mult, op1=ALU.mult)),
                                (kvcn_B, kvnw_B, kvrb_B), (kvcn_B,))
                        dve(lambda e: e.tensor_copy(out=kpebf_s[:],
                                                    in_=kpet_s[:, 0:KG]),
                            (kpet_B,), (kpebf_B,))
                        pe([lambda e: e.matmul(pb[3][0:DR, :], swap_s[:],
                                               kpebf_s[:], start=True,
                                               stop=True)],
                           [swap_B, kpebf_B], [pbB[3]])
                        act(lambda e: e.activation(
                            out=kpet_s[:, KG:2 * KG], in_=pb[3][0:DR, :],
                            func=AF.Copy), (pbB[3],), (kpet_B,))
                        dve((lambda e, t0=t0: e.tensor_tensor(
                            out=kpet_s[:, 2 * KG:3 * KG], in0=kpet_s[:, 0:KG],
                            in1=cosek_s[:, t0:t0 + KG], op=ALU.mult)),
                            (kpet_B, cosek_B), (kpet_B,))
                        dve((lambda e, t0=t0: e.tensor_tensor(
                            out=kpet_s[:, KG:2 * KG],
                            in0=kpet_s[:, KG:2 * KG],
                            in1=sinek_s[:, t0:t0 + KG], op=ALU.mult)),
                            (kpet_B, sinek_B), (kpet_B,))
                        dve((lambda e, t0=t0: e.tensor_tensor(
                            out=kpe_s[:, t0:t0 + KG],
                            in0=kpet_s[:, 2 * KG:3 * KG],
                            in1=kpet_s[:, KG:2 * KG], op=ALU.add)),
                            (kpet_B,), (kpe_B,))
                        dve((lambda e, t0=t0: e.tensor_tensor(
                            out=kpebf_s[:], in0=kpe_s[:, t0:t0 + KG],
                            in1=kpe_s[:, t0:t0 + KG], op=ALU.mult)),
                            (kpe_B,), (kpebf_B,))
                        pe([lambda e: e.matmul(pb[3][0:1, :],
                                               onescb_s[0:DR, :], kpebf_s[:],
                                               start=True, stop=True)],
                           [onescb_B, kpebf_B], [pbB[3]])
                        act((lambda e, t0=t0: e.activation(
                            out=kpesq_s[:, t0:t0 + KG], in_=pb[3][0:1, :],
                            func=AF.Copy)), (pbB[3],), (kpesq_B,))
                    p.fence()

                qan_s, qan_B = sb(s1, "qan", (128, nQLT * TPC), F16)
                s2b = ExitStack()
                with s2b:
                    xnTown_s, xnTown_B = sb(s2b, "xnTown",
                                            (128, nKT * TPC), F16)
                    dma("sync", lambda e: e.dma_start(
                        out=xnTown_s[:].rearrange("q (kt t) -> q kt t",
                                                  kt=nKT),
                        in_=xnT_scr.rearrange("(kt q) t -> q kt t", q=128)),
                        (xnT_scr_B,), (xnTown_B,))
                    wqa_s, wqa_B = sb(s2b, "wqa", (128, nKT * 256), F16)
                    qstat_s, qstat_B = sb(s2b, "qstat", (1, TPC), F32)
                    qrb_s, qrb_B = sb(s2b, "qrb", (128, TPC), F32)
                    for mp in range(nQLT // 2):
                        dma("gpsimd", (lambda e, mp=mp: e.dma_start(
                            out=wqa_s[:].rearrange("q (kt m) -> q kt m",
                                                   kt=nKT),
                            in_=wqaT.rearrange("(kt q) m -> q kt m", q=128)[
                                :, :, mp * 256:(mp + 1) * 256])),
                            (wqa_B,), (wqa_B,))
                        for mi in range(2):
                            mt = 2 * mp + mi
                            for nt in range(nNT):
                                bank = pb[(2 * mi + nt) % 4]
                                bankB = pbB[(2 * mi + nt) % 4]
                                fns = []
                                for kt in range(nKT):
                                    lhsT = wqa_s[:, kt * 256 + mi * 128:
                                                 kt * 256 + mi * 128 + 128]
                                    rhs = xnTown_s[:, kt * TPC + nt * 512:
                                                   kt * TPC + nt * 512 + 512]
                                    fns.append((lambda e, lhsT=lhsT, rhs=rhs,
                                                bank=bank, first=(kt == 0),
                                                last=(kt == nKT - 1):
                                                e.matmul(bank[:], lhsT, rhs,
                                                         start=first,
                                                         stop=last)))
                                pe(fns, [wqa_B, xnTown_B], [bankB])
                                act((lambda e, bank=bank, mt=mt, nt=nt:
                                     e.activation(
                                         out=qan_s[:, mt * TPC + nt * 512:
                                                   mt * TPC + nt * 512 + 512],
                                         in_=bank[:], func=AF.Copy)),
                                    [bankB], [qan_B])
                                act((lambda e, bank=bank: e.activation(
                                    out=sqbuf_s[:, 0:512], in_=bank[:],
                                    func=AF.Square)), [bankB], [sqbuf_B])
                                pe([(lambda e, mt=mt, nt=nt: e.matmul(
                                    pb[6 + nt][0:1, :], onescb_s[:],
                                    sqbuf_s[:, 0:512],
                                    start=(mt == 0),
                                    stop=(mt == nQLT - 1)))],
                                    [sqbuf_B, onescb_B], [pbB[6 + nt]])
                    for nt in range(nNT):
                        dve((lambda e, nt=nt: e.tensor_scalar(
                            out=qstat_s[:, nt * 512:(nt + 1) * 512],
                            in0=pb[6 + nt][0:1, :], scalar1=1.0 / QL,
                            scalar2=EPS_RMS, op0=ALU.mult, op1=ALU.add)),
                            (pbB[6 + nt],), (qstat_B,))
                    act(lambda e: e.activation(out=qstat_s[:], in_=qstat_s[:],
                                               func=AF.Ln),
                        (qstat_B,), (qstat_B,))
                    act(lambda e: e.activation(out=qstat_s[:], in_=qstat_s[:],
                                               func=AF.Exp, scale=-0.5),
                        (qstat_B,), (qstat_B,))
                    for nt in range(nNT):
                        pe([(lambda e, nt=nt: e.matmul(
                            pb[4][:], ones1f_s[:],
                            qstat_s[:, nt * 512:(nt + 1) * 512],
                            start=True, stop=True))],
                           [ones1f_B, qstat_B], [pbB[4]])
                        dve((lambda e, nt=nt: e.tensor_copy(
                            out=qrb_s[:, nt * 512:(nt + 1) * 512],
                            in_=pb[4][:])), (pbB[4],), (qrb_B,))
                    for mt in range(nQLT):
                        dve((lambda e, mt=mt: e.scalar_tensor_tensor(
                            out=qan_s[:, mt * TPC:(mt + 1) * TPC],
                            in0=qan_s[:, mt * TPC:(mt + 1) * TPC],
                            scalar=qnw_s[:, mt:mt + 1], in1=qrb_s[:],
                            op0=ALU.mult, op1=ALU.mult)),
                            (qan_B, qnw_B, qrb_B), (qan_B,))
                p.fence()

            if tapout("qan", (128, nQLT * TPC), F16):
                dma("sync", lambda e: e.dma_start(out=tap_aps["qan"],
                                                  in_=qan_s[:]),
                    (qan_B,), (out_B,))
            if tapout("kvcn", (128, nKLT * S), F16):
                dma("sync", lambda e: e.dma_start(out=tap_aps["kvcn"],
                                                  in_=kvcn_s[:]),
                    (kvcn_B,), (out_B,))
            if tapout("kpe", (DR, S), F16):
                dma("sync", lambda e: e.dma_start(out=tap_aps["kpe"],
                                                  in_=kpe_s[:]),
                    (kpe_B,), (out_B,))

            s3 = ExitStack()
            with s3:
                wqb2 = [sb(s3, f"wqb{i}", (128, nQLT * HPG * DH), F16)
                        for i in range(2)]
                wkvb2 = [sb(s3, f"wkvb{i}", (128, nKLT * HPG * DKV), F16)
                         for i in range(2)]
                qTn2 = [sb(s3, f"qTn{i}", (128, TPC), F16) for i in range(2)]
                qTr2 = [sb(s3, f"qTr{i}", (DR, TPC), F16) for i in range(2)]
                qrt_s, qrt_B = sb(s3, "qrt", (DR, 2 * 512), F16)
                qn_s, qn_B = sb(s3, "qn", (1, TPC), F32)
                nmrow2 = [sb(s3, f"nmrow{i}", (1, TPC), F32)
                          for i in range(2)]
                kn2 = [sb(s3, f"kn{i}", (128, S), F16) for i in range(2)]
                k2_s, k2_B = sb(s3, "k2", (1, S), F32)
                kmx_s, kmx_B = sb(s3, "kmx", (1, 2), F32)
                v42 = [sb(s3, f"v4{i}", (128, (S // 128) * HPG * 128),
                          BF16) for i in range(2)]
                atth2 = [sb(s3, f"atth{i}", (128, TPC), F16)
                         for i in range(2)]
                probs = [sb(s3, f"probs{i}", (128, KG), BF16)
                         for i in range(3)]
                probsT = [sb(s3, f"probsT{i}", (128, 512), BF16)
                          for i in range(2)]
                sums2 = [sb(s3, f"sums{i}", (128, 8), F32)
                         for i in range(nJT)]
                negm2 = [sb(s3, f"negm{i}", (128, nJT), F32)
                         for i in range(2)]
                inv_s, inv_B = sb(s3, "inv", (128, 1), F32)
                invr_s, invr_B = sb(s3, "invr", (1, 128), F32)
                invb_s, invb_B = sb(s3, "invb", (128, 128), F32)

                heads = [(g_, h4_) for g_ in range(nHG)
                         for h4_ in range(HPG)]
                pb4hB = [pbB[4], Buf("pb4h1")]
                pqctr = [0]

                def pbq():
                    pqctr[0] ^= 1
                    return pb[6 + pqctr[0]], pbB[6 + pqctr[0]]

                def group_dmas(g):
                    wqb_t, wqb_B = wqb2[g % 2]
                    wkvb_t, wkvb_B = wkvb2[g % 2]
                    dma("gpsimd", (lambda e, g=g, wqb_t=wqb_t: e.dma_start(
                        out=wqb_t[:].rearrange("q (kt m) -> q kt m", kt=nQLT),
                        in_=wqbT.rearrange("(kt q) m -> q kt m", q=128)[
                            :, :, g * HPG * DH:(g + 1) * HPG * DH])),
                        (wqb_B,), (wqb_B,))
                    dma("gpsimd", (lambda e, g=g, wkvb_t=wkvb_t: e.dma_start(
                        out=wkvb_t[:].rearrange("q (kt m) -> q kt m",
                                                kt=nKLT),
                        in_=wkvbT.rearrange("(kt q) m -> q kt m", q=128)[
                            :, :, g * HPG * DKV:(g + 1) * HPG * DKV])),
                        (wkvb_B,), (wkvb_B,))

                def prelude_gen(idx):
                    g, h4 = heads[idx]
                    h = g * HPG + h4
                    par_h = h % 2
                    qTn_s, qTn_B = qTn2[par_h]
                    qTr_s, qTr_B = qTr2[par_h]
                    nmrow_s, nmrow_B = nmrow2[par_h]
                    kn_s, kn_B = kn2[par_h]
                    wqb_t, wqb_B = wqb2[g % 2]
                    wkvb_t, wkvb_B = wkvb2[g % 2]
                    wqbv = wqb_t[:].rearrange("q (kt h m) -> q kt h m",
                                              kt=nQLT, h=HPG)
                    wkvbv = wkvb_t[:].rearrange("q (kt h m) -> q kt h m",
                                                kt=nKLT, h=HPG)
                    if h4 == 2 and g + 1 < nHG:
                        group_dmas(g + 1)
                    pend = []

                    def flush():
                        for c in pend:
                            c()
                        pend.clear()
                    if h4 == 0:
                        v4_s, v4_B = v42[g % 2]
                        for kt in range(S // 128):
                            bank, bankB = pbq()
                            fns = []
                            for klt in range(nKLT):
                                lhsT = kvcn_s[:, klt * S + kt * 128:
                                              klt * S + kt * 128 + 128]
                                rhs = wkvbv[:, klt, :, DN:DN + DV]
                                fns.append((lambda e, lhsT=lhsT, rhs=rhs,
                                            bank=bank, first=(klt == 0),
                                            last=(klt == nKLT - 1):
                                            e.matmul(bank[:], lhsT, rhs,
                                                     start=first,
                                                     stop=last)))
                            pe(fns, [wkvb_B, kvcn_B], [bankB])
                            act((lambda e, bank=bank, kt=kt, v4_s=v4_s:
                                 e.activation(
                                     out=v4_s[:, kt * HPG * 128:
                                              (kt + 1) * HPG * 128],
                                     in_=bank[:, 0:HPG * 128],
                                     func=AF.Copy)), [bankB], [v4_B])
                            if kt % 2 == 1:
                                yield
                    for sub in range(2):
                        moff = sub * 128
                        mw = 128 if sub == 0 else DH - 128
                        for nt in range(nNT):
                            bank, bankB = pbq()
                            flush()
                            fns = []
                            for kt in range(nQLT):
                                lhsT = wqbv[:, kt, h4, moff:moff + mw]
                                rhs = qan_s[:, kt * TPC + nt * 512:
                                            kt * TPC + nt * 512 + 512]
                                fns.append((lambda e, lhsT=lhsT, rhs=rhs,
                                            bank=bank, mw=mw,
                                            first=(kt == 0),
                                            last=(kt == nQLT - 1):
                                            e.matmul(bank[0:mw, :], lhsT,
                                                     rhs, start=first,
                                                     stop=last)))
                            pe(fns, [wqb_B, qan_B], [bankB])
                            if sub == 0:
                                act((lambda e, bank=bank, nt=nt,
                                     qTn_s=qTn_s:
                                     e.activation(
                                         out=qTn_s[:, nt * 512:
                                                   nt * 512 + 512],
                                         in_=bank[:], func=AF.Copy,
                                         scale=scale)), [bankB], [qTn_B])
                            else:
                                act((lambda e, bank=bank, nt=nt, mw=mw,
                                     qTr_s=qTr_s:
                                     e.activation(
                                         out=qTr_s[:, nt * 512:
                                                   nt * 512 + 512],
                                         in_=bank[0:mw, :], func=AF.Copy,
                                         scale=scale)), [bankB], [qTr_B])
                            act((lambda e, bank=bank, mw=mw:
                                 e.activation(
                                     out=sqbuf_s[0:mw, 0:512],
                                     in_=bank[0:mw, :], func=AF.Square)),
                                [bankB], [sqbuf_B])
                            def _qrow(sub=sub, nt=nt, mw=mw):
                                pe([(lambda e, mw=mw: e.matmul(
                                    pb[3][0:1, :], onescb_s[0:mw, :],
                                    sqbuf_s[0:mw, 0:512], start=True,
                                    stop=True))],
                                    [sqbuf_B, onescb_B], [pbB[3]])
                                if sub == 0:
                                    act((lambda e, nt=nt: e.activation(
                                        out=qn_s[:, nt * 512:
                                                 nt * 512 + 512],
                                        in_=pb[3][0:1, :], func=AF.Copy)),
                                        (pbB[3],), (qn_B,))
                                else:
                                    dve((lambda e, nt=nt: e.tensor_tensor(
                                        out=qn_s[:, nt * 512:
                                                 nt * 512 + 512],
                                        in0=pb[3][0:1, :],
                                        in1=qn_s[:, nt * 512:
                                                 nt * 512 + 512],
                                        op=ALU.add)), (pbB[3], qn_B),
                                        (qn_B,))
                            pend.append(_qrow)
                            yield
                    for nt in range(nNT):
                        t0q = nt * 512
                        bank, bankB = pbq()
                        flush()
                        pe([(lambda e, t0q=t0q, bank=bank, qTr_s=qTr_s:
                             e.matmul(
                                 bank[0:DR, :], swap_s[:],
                                 qTr_s[:, t0q:t0q + 512], start=True,
                                 stop=True))], [swap_B, qTr_B], [bankB])
                        act((lambda e, bank=bank: e.activation(
                            out=qrt_s[:, 512:1024], in_=bank[0:DR, :],
                            func=AF.Copy)), (bankB,), (qrt_B,))
                        dve((lambda e, t0q=t0q, qTr_s=qTr_s:
                             e.tensor_tensor(
                                 out=qrt_s[:, 0:512],
                                 in0=qTr_s[:, t0q:t0q + 512],
                                 in1=coseq_s[:, t0q:t0q + 512],
                                 op=ALU.mult)), (qTr_B, coseq_B),
                            (qrt_B,))
                        dve((lambda e, t0q=t0q: e.tensor_tensor(
                            out=qrt_s[:, 512:1024],
                            in0=qrt_s[:, 512:1024],
                            in1=sineq_s[:, t0q:t0q + 512],
                            op=ALU.mult)), (qrt_B, sineq_B), (qrt_B,))
                        dve((lambda e, t0q=t0q, qTr_s=qTr_s:
                             e.tensor_tensor(
                                 out=qTr_s[:, t0q:t0q + 512],
                                 in0=qrt_s[:, 0:512],
                                 in1=qrt_s[:, 512:1024],
                                 op=ALU.add)), (qrt_B,), (qTr_B,))
                    yield
                    act(lambda e: e.activation(
                        out=qn_s[:], in_=qn_s[:], func=AF.Ln),
                        (qn_B,), (qn_B,))
                    act(lambda e: e.activation(
                        out=qn_s[:], in_=qn_s[:], func=AF.Exp,
                        scale=0.5), (qn_B,), (qn_B,))
                    dve(lambda e: e.tensor_scalar(
                        out=qn_s[:], in0=qn_s[:], scalar1=scale,
                        scalar2=None, op0=ALU.mult), (qn_B,), (qn_B,))
                    for nt in range(S // 512):
                        bank, bankB = pbq()
                        flush()
                        fns = []
                        for klt in range(nKLT):
                            lhsT = wkvbv[:, klt, h4, 0:DN]
                            rhs = kvcn_s[:, klt * S + nt * 512:
                                         klt * S + nt * 512 + 512]
                            fns.append((lambda e, lhsT=lhsT, rhs=rhs,
                                        bank=bank, first=(klt == 0),
                                        last=(klt == nKLT - 1):
                                        e.matmul(bank[:], lhsT, rhs,
                                                 start=first, stop=last)))
                        pe(fns, [wkvb_B, kvcn_B], [bankB])
                        act((lambda e, bank=bank, nt=nt, kn_s=kn_s:
                             e.activation(
                                 out=kn_s[:, nt * 512:nt * 512 + 512],
                                 in_=bank[:], func=AF.Copy)),
                            [bankB], [kn_B])
                        act((lambda e, bank=bank: e.activation(
                            out=sqbuf_s[:, 0:512], in_=bank[:],
                            func=AF.Square)), [bankB], [sqbuf_B])

                        def _krow(nt=nt):
                            pe([lambda e: e.matmul(
                                pb[3][0:1, :], onescb_s[:],
                                sqbuf_s[:, 0:512], start=True, stop=True)],
                                [sqbuf_B, onescb_B], [pbB[3]])
                            dve((lambda e, nt=nt: e.tensor_tensor(
                                out=k2_s[:, nt * 512:(nt + 1) * 512],
                                in0=pb[3][0:1, :],
                                in1=kpesq_s[:, nt * 512:(nt + 1) * 512],
                                op=ALU.add)), (pbB[3], kpesq_B), (k2_B,))
                        pend.append(_krow)
                        yield
                    flush()
                    dve(lambda e: e.reduce_max(out=kmx_s[:, 0:1],
                                               in_=k2_s[:], axis=AX),
                        (k2_B,), (kmx_B,))
                    act(lambda e: e.activation(
                        out=kmx_s[:, 0:1], in_=kmx_s[:, 0:1], func=AF.Ln),
                        (kmx_B,), (kmx_B,))
                    act(lambda e: e.activation(
                        out=kmx_s[:, 0:1], in_=kmx_s[:, 0:1], func=AF.Exp,
                        scale=0.5), (kmx_B,), (kmx_B,))
                    dve((lambda e, nmrow_s=nmrow_s: e.tensor_scalar(
                        out=nmrow_s[:], in0=qn_s[:],
                        scalar1=kmx_s[:, 0:1], scalar2=-1.0,
                        op0=ALU.mult, op1=ALU.mult)),
                        (qn_B, kmx_B), (nmrow_B,))
                    negm_s, negm_B = negm2[par_h]
                    fns = []
                    for j in range(nJT):
                        fns.append((lambda e, j=j, nmrow_s=nmrow_s:
                                    e.matmul(
                                        pb[3][0:128, j:j + 1],
                                        nmrow_s[:, j * 128:(j + 1) * 128],
                                        ones1f_s[:, 0:1], start=(j == 0),
                                        stop=(j == nJT - 1))))
                    pe(fns, [nmrow_B, ones1f_B], [pbB[3]])
                    act((lambda e, negm_s=negm_s: e.activation(
                        out=negm_s[:], in_=pb[3][0:128, 0:nJT],
                        func=AF.Copy)), (pbB[3],), (negm_B,))

                group_dmas(0)
                for _ in prelude_gen(0):
                    pass
                for idx in range(len(heads)):
                    g, h4 = heads[idx]
                    h = g * HPG + h4
                    par_h = h % 2
                    qTn_s, qTn_B = qTn2[par_h]
                    qTr_s, qTr_B = qTr2[par_h]
                    kn_s, kn_B = kn2[par_h]
                    negm_s, negm_B = negm2[par_h]
                    atth_s, atth_B = atth2[par_h]
                    v4_s, v4_B = v42[g % 2]
                    tasks = [(j, kg) for j in range(nJT)
                             for kg in range(cnt[j])]

                    def emit_scores(ti):
                        j, kg = tasks[ti]
                        bank, bankB = pb[ti % 2], pbB[ti % 2]
                        fns = [
                            (lambda e, j=j, kg=kg, bank=bank,
                             qTn_s=qTn_s, kn_s=kn_s:
                             e.matmul(bank[:],
                                      qTn_s[:, j * 128:j * 128 + 128],
                                      kn_s[:, kg * KG:(kg + 1) * KG],
                                      start=True, stop=False)),
                            (lambda e, j=j, kg=kg, bank=bank,
                             qTr_s=qTr_s:
                             e.matmul(bank[:],
                                      qTr_s[:, j * 128:j * 128 + 128],
                                      kpe_s[:, kg * KG:(kg + 1) * KG],
                                      start=False, stop=True)),
                        ]
                        pe(fns, [qTn_B, qTr_B, kn_B, kpe_B], [bankB])

                    srcs = {}

                    def emit_exp(ti):
                        j, kg = tasks[ti]
                        cj = cnt[j]
                        diag = (kg == cj - 1)
                        bank, bankB = pb[ti % 2], pbB[ti % 2]
                        sums_s, sums_B = sums2[j]
                        pr_t, pr_B = probs[ti % 3]
                        if not diag:
                            act((lambda e, bank=bank, pr_t=pr_t, kg=kg,
                                 j=j, negm_s=negm_s, sums_s=sums_s:
                                 e.activation(
                                     out=pr_t[:], in_=bank[:],
                                     func=AF.Exp,
                                     bias=negm_s[:, j:j + 1],
                                     accum_out=sums_s[:, kg:kg + 1])),
                                [bankB, negm_B], [pr_B, sums_B])
                            srcs[ti] = (pr_t, pr_B)
                        else:
                            act((lambda e, bank=bank, pr_t=pr_t, j=j,
                                 negm_s=negm_s: e.activation(
                                     out=pr_t[:], in_=bank[:],
                                     func=AF.Exp,
                                     bias=negm_s[:, j:j + 1])),
                                [bankB, negm_B], [pr_B])
                            dve((lambda e, pr_t=pr_t, j=j, kg=kg,
                                 sums_s=sums_s: e.scalar_tensor_tensor(
                                     out=pr_t[:], in0=pr_t[:],
                                     scalar=1.0,
                                     in1=masks_s[:, j * KG:(j + 1) * KG],
                                     op0=ALU.mult, op1=ALU.mult,
                                     accum_out=sums_s[:, kg:kg + 1])),
                                (pr_B, masks_B), (pr_B, sums_B))
                            srcs[ti] = (pr_t, pr_B)

                    def emit_tpv(ti):
                        j, kg = tasks[ti]
                        cj = cnt[j]
                        diag = (kg == cj - 1)
                        pvb, pvbB = (pb[5], pbB[5]) if j % 2 == 0 else \
                            (pb[2], pbB[2])
                        sums_s, sums_B = sums2[j]
                        src_t, src_B = srcs.pop(ti)
                        half = (ti % 2) * 512
                        hB = pb4hB[ti % 2]
                        fns = []
                        for ks in range(4):
                            fns.append((lambda e, src_t=src_t, ks=ks,
                                        half=half:
                                        e.transpose(
                                            pb[4][:].bitcast(BF16)[
                                                :, half + ks * 128:
                                                half + (ks + 1) * 128],
                                            src_t[:, ks * 128:
                                                  (ks + 1) * 128],
                                            identb_s[:])))
                        pe(fns, [src_B, identb_B], [hB])
                        pT_t, pT_B = probsT[ti % 2]
                        dve((lambda e, pT_t=pT_t, half=half: e.tensor_copy(
                            out=pT_t[:],
                            in_=pb[4][:].bitcast(BF16)[:, half:half + 512])),
                            (hB,), (pT_B,))
                        fns = []
                        for ks in range(4):
                            fns.append(
                                (lambda e, kg=kg, ks=ks, h4=h4,
                                 pT_t=pT_t, cj=cj, pvb=pvb, v4_s=v4_s:
                                 e.matmul(
                                     pvb[:, 0:128],
                                     v4_s[:, (kg * 4 + ks) * HPG * 128
                                          + h4 * 128:
                                          (kg * 4 + ks) * HPG * 128
                                          + h4 * 128 + 128],
                                     pT_t[:, ks * 128:(ks + 1) * 128],
                                     start=(kg == 0 and ks == 0),
                                     stop=(kg == cj - 1 and ks == 3))))
                        pe(fns, [v4_B, pT_B], [pvbB])
                        if diag:
                            dve((lambda e, cj=cj, sums_s=sums_s:
                                 e.tensor_reduce(
                                     out=inv_s[:], in_=sums_s[:, 0:cj],
                                     axis=AX, op=ALU.add)),
                                (sums_B,), (inv_B,))
                            dve(lambda e: e.tensor_scalar(
                                out=inv_s[:], in0=inv_s[:], scalar1=1e-30,
                                scalar2=None, op0=ALU.add),
                                (inv_B,), (inv_B,))
                            dve(lambda e: e.reciprocal(out=inv_s[:],
                                                       in_=inv_s[:]),
                                (inv_B,), (inv_B,))
                            pe([lambda e: e.matmul(
                                pb[3][0:1, 0:128], inv_s[:], identf_s[:],
                                start=True, stop=True)],
                               [inv_B, identf_B], [pbB[3]])
                            act(lambda e: e.activation(
                                out=invr_s[:], in_=pb[3][0:1, 0:128],
                                func=AF.Copy), (pbB[3],), (invr_B,))
                            pe([lambda e: e.matmul(
                                pb[3][:, 0:128], ones1f_s[:], invr_s[:],
                                start=True, stop=True)],
                               [invr_B, ones1f_B], [pbB[3]])
                            act(lambda e: e.activation(
                                out=invb_s[:], in_=pb[3][:, 0:128],
                                func=AF.Copy), (pbB[3],), (invb_B,))
                            dve((lambda e, j=j, pvb=pvb, atth_s=atth_s:
                                 e.tensor_tensor(
                                     out=atth_s[:, j * 128:
                                                j * 128 + 128],
                                     in0=pvb[:, 0:128], in1=invb_s[:],
                                     op=ALU.mult)),
                                (pvbB, invb_B), (atth_B,))

                    ngen = (prelude_gen(idx + 1)
                            if idx + 1 < len(heads) else iter(()))
                    nT = len(tasks)
                    for ti in range(nT):
                        emit_scores(ti)
                        if ti > 0:
                            emit_exp(ti - 1)
                        if ti > 2:
                            emit_tpv(ti - 3)
                        next(ngen, None)
                    emit_exp(nT - 1)
                    for tl in range(max(0, nT - 3), nT):
                        emit_tpv(tl)
                    for _ in ngen:
                        pass
                    dma("sync", (lambda e, h=h, atth_s=atth_s:
                                 e.dma_start(
                                     out=attn_scr[h * 128:(h + 1) * 128,
                                                  :],
                                     in_=atth_s[:])),
                        (atth_B,), (attn_scr_B,))
                p.fence()

        if tapout("attn", (NH * 128, TPC), F16):
            dma("sync", lambda e: e.dma_start(out=tap_aps["attn"],
                                              in_=attn_scr),
                (attn_scr_B,), (out_B,))

        s4 = ExitStack()
        with s4:
            woT_b = [sb(s4, f"woTb{i}", (128, nKo * 512), F16)
                     for i in range(2)]
            acol = [sb(s4, f"acol{i}", (128, nKo * 128), F16)
                    for i in range(2)]
            wost_s, wost_B = sb(s4, "wost", (128, nJT * 512), F16)
            for nw in range(H // 512):
                wt, wtB = woT_b[nw % 2]
                dma("gpsimd", (lambda e, nw=nw, wt=wt: e.dma_start(
                    out=wt[:].rearrange("q (kt m) -> q kt m", kt=nKo),
                    in_=woT.rearrange("(kt q) m -> q kt m", q=128)[
                        :, :, nw * 512:(nw + 1) * 512])), (wtB,), (wtB,))
                for t in range(nJT):
                    ac, acB = acol[t % 2]
                    dma("sync", (lambda e, t=t, ac=ac: e.dma_start(
                        out=ac[:].rearrange("q (kt m) -> q kt m", kt=nKo),
                        in_=attn_scr.rearrange("(kt q) m -> q kt m", q=128)[
                            :, :, t * 128:(t + 1) * 128])),
                        (acB, attn_scr_B), (acB,))
                    bank, bankB = pb[t % 2], pbB[t % 2]
                    fns = []
                    for kt in range(nKo):
                        lhsT = ac[:, kt * 128:(kt + 1) * 128]
                        rhs = wt[:, kt * 512:(kt + 1) * 512]
                        fns.append((lambda e, lhsT=lhsT, rhs=rhs, bank=bank,
                                    first=(kt == 0), last=(kt == nKo - 1):
                                    e.matmul(bank[:], lhsT, rhs,
                                             start=first, stop=last)))
                    pe(fns, [acB, wtB], [bankB])
                    act((lambda e, bank=bank, t=t: e.activation(
                        out=wost_s[:, t * 512:(t + 1) * 512], in_=bank[:],
                        func=AF.Copy)), [bankB], [wost_B])
                dma("sync", (lambda e, nw=nw: e.dma_start(
                    out=wo_scr.rearrange("(tt q) c -> q tt c", q=128)[
                        :, :, nw * 512:(nw + 1) * 512],
                    in_=wost_s[:].rearrange("q (tt c) -> q tt c", tt=nJT))),
                    (wost_B,), (wo_scr_B,))
            p.fence()

        s5 = ExitStack()
        with s5:
            xnTo2_s, xnTo2_B = sb(s5, "xnTo2", (128, nKT * TPC), F16)
            dma("sync", lambda e: e.dma_start(
                out=xnTo2_s[:].rearrange("q (kt t) -> q kt t", kt=nKT),
                in_=xnT_scr.rearrange("(kt q) t -> q kt t", q=128)),
                (xnT_scr_B,), (xnTo2_B,))
            s5a = ExitStack()
            with s5a:
                for tg in range(nTG):
                    fns = []
                    for kt in range(nKT):
                        kp = kt // 8
                        fns.append((lambda e, kt=kt, kp=kp, tg=tg: e.matmul(
                            pb[2][0:nG, :], b32_s[:, kp * nG:(kp + 1) * nG],
                            xnTo2_s[:, kt * TPC + tg * 512:
                                    kt * TPC + tg * 512 + 512],
                            start=(kt == 0), stop=(kt == nKT - 1))))
                    pe(fns, [b32_B, xnTo2_B], [pbB[2]])
                    act((lambda e, tg=tg: e.activation(
                        out=xs_s[:, tg * 512:(tg + 1) * 512],
                        in_=pb[2][0:nG, :], func=AF.Copy)), [pbB[2]], [xs_B])
                p.fence()

            slab2 = [sb(s5, f"slab{i}", (128, 1024), U32) for i in range(2)]
            uscr_s, uscr_B = sb(s5, "uscr", (128, 2048), U16)
            planes_s, planes_B = sb(s5, "planes", (128, nKpack * 4 * 2048),
                                    F16)
            scgwb_s, scgwb_B = sb(s5, "scgwb", (nG, 1024), F16)
            scgwbn_s, scgwbn_B = sb(s5, "scgwbn", (nG, 1024), F16)
            qzgw_s, qzgw_B = sb(s5, "qzgw", (nG, 128), U32)
            zq8w_s, zq8w_B = sb(s5, "zq8w", (nG, 128), U16)
            zfold_s, zfold_B = sb(s5, "zfold", (nG, 8 * 128), F16)
            scb_s, scb_B = sb(s5, "scb", (128, nKpack * 2048), F16)
            gtmp_s, gtmp_B = sb(s5, "gtmp", (128, 8 * nTG * 512), F16)
            hstage = [sb(s5, f"hstage{i}", (128, 8 * 512), F16)
                      for i in range(nTG)]

            for P in range(nPd):
                win = gupos[8 * (P + 1)] - gupos[8 * P]
                wT = _dwidth(d, P)
                for region in range(2):
                    qsrc = qwg_g if region == 0 else qwg_u
                    ssrc = scg_g if region == 0 else scg_u
                    dma("gpsimd", (lambda e, P=P, ssrc=ssrc, win=win:
                                   e.dma_start(
                                       out=scgwb_s[:, 0:win],
                                       in_=ssrc[:, gupos[8 * P]:
                                                gupos[8 * P] + win])),
                        (scgwb_B,), (scgwb_B,))
                    dve((lambda e, win=win: e.tensor_scalar(
                        out=scgwbn_s[:, 0:win], in0=scgwb_s[:, 0:win],
                        scalar1=-1.0, scalar2=None, op0=ALU.mult)),
                        (scgwb_B,), (scgwbn_B,))
                    dma("sync", (lambda e, P=P, region=region, wT=wT:
                                 e.dma_start(
                                     out=qzgw_s[:, 0:wT],
                                     in_=qzg[:, region * (FI // 8) + 128 * P:
                                             region * (FI // 8) + 128 * P
                                             + wT])), (qzgw_B,), (qzgw_B,))
                    for ip in range(8):
                        half, kq = ip // 4, ip % 4
                        if kq == 0:
                            fzq = (lambda e, half=half, wT=wT:
                                   e.tensor_scalar(
                                       out=zq8w_s[:, 0:wT],
                                       in0=qzgw_s[:, 0:wT].bitcast(U16)[
                                           :, half:half + 2 * wT - 1:2],
                                       scalar1=15, scalar2=None,
                                       op0=ALU.bitwise_and))
                        elif kq == 3:
                            fzq = (lambda e, half=half, wT=wT:
                                   e.tensor_scalar(
                                       out=zq8w_s[:, 0:wT],
                                       in0=qzgw_s[:, 0:wT].bitcast(U16)[
                                           :, half:half + 2 * wT - 1:2],
                                       scalar1=12, scalar2=None,
                                       op0=ALU.logical_shift_right))
                        else:
                            fzq = (lambda e, half=half, kq=kq, wT=wT:
                                   e.tensor_scalar(
                                       out=zq8w_s[:, 0:wT],
                                       in0=qzgw_s[:, 0:wT].bitcast(U16)[
                                           :, half:half + 2 * wT - 1:2],
                                       scalar1=4 * kq, scalar2=15,
                                       op0=ALU.logical_shift_right,
                                       op1=ALU.bitwise_and))
                        dve(fzq, (qzgw_B,), (zq8w_B,))
                        dve((lambda e, ip=ip, wT=wT:
                             e.scalar_tensor_tensor(
                                 out=zfold_s[:, ip * wT:(ip + 1) * wT],
                                 in0=zq8w_s[:, 0:wT], scalar=1.0,
                                 in1=scgwbn_s[:, ip * wT:(ip + 1) * wT],
                                 op0=ALU.add, op1=ALU.mult)),
                            (zq8w_B, scgwbn_B), (zfold_B,))
                    for kp in range(nKpack):
                        for cs in range(0, win, 256):
                            cw = min(256, win - cs)
                            pe([(lambda e, kp=kp, cs=cs, cw=cw: e.matmul(
                                pb[4][:, 0:2 * cw],
                                eg_s[:, kp * 128:(kp + 1) * 128],
                                scgwb_s[:, cs:cs + cw].unsqueeze(2)
                                .broadcast_to([nG, cw, 2]), start=True,
                                stop=True))], [eg_B, scgwb_B], [pbB[4]])
                            act((lambda e, kp=kp, cs=cs, cw=cw: e.activation(
                                out=scb_s[:, kp * 2048 + 2 * cs:
                                          kp * 2048 + 2 * cs + 2 * cw],
                                in_=pb[4][:, 0:2 * cw], func=AF.Copy)),
                                [pbB[4]], [scb_B])
                        slab_s, slab_B = slab2[kp % 2]
                        dma("sync", (lambda e, P=P, kp=kp, qsrc=qsrc, win=win,
                                     slab_s=slab_s: e.dma_start(
                                         out=slab_s[:, 0:win],
                                         in_=qsrc[kp * 128:(kp + 1) * 128,
                                                  gupos[8 * P]:
                                                  gupos[8 * P] + win])),
                            (slab_B,), (slab_B,))
                        for k in range(4):
                            if k == 0:
                                fn = (lambda e, win=win, slab_s=slab_s:
                                      e.tensor_scalar(
                                          out=uscr_s[:, 0:2 * win],
                                          in0=slab_s[:, 0:win].bitcast(U16),
                                          scalar1=15, scalar2=None,
                                          op0=ALU.bitwise_and))
                            elif k == 3:
                                fn = (lambda e, win=win, slab_s=slab_s:
                                      e.tensor_scalar(
                                          out=uscr_s[:, 0:2 * win],
                                          in0=slab_s[:, 0:win].bitcast(U16),
                                          scalar1=12, scalar2=None,
                                          op0=ALU.logical_shift_right))
                            else:
                                fn = (lambda e, win=win, k=k, slab_s=slab_s:
                                      e.tensor_scalar(
                                          out=uscr_s[:, 0:2 * win],
                                          in0=slab_s[:, 0:win].bitcast(U16),
                                          scalar1=4 * k, scalar2=15,
                                          op0=ALU.logical_shift_right,
                                          op1=ALU.bitwise_and))
                            dve(fn, (slab_B,), (uscr_B,))
                            dve((lambda e, kp=kp, k=k, win=win:
                                 e.tensor_tensor(
                                     out=planes_s[:, (kp * 4 + k) * 2048:
                                                  (kp * 4 + k) * 2048
                                                  + 2 * win],
                                     in0=uscr_s[:, 0:2 * win],
                                     in1=scb_s[:, kp * 2048:
                                               kp * 2048 + 2 * win],
                                     op=ALU.mult)),
                                (uscr_B, scb_B), (planes_B,))
                    for ip in range(8):
                        for tg in range(nTG):
                            bank = pb[2 * (ip % 2) + (tg % 2)]
                            bankB = pbB[2 * (ip % 2) + (tg % 2)]
                            mms = []
                            for kp in range(nKpack):
                                for half in range(2):
                                    for k in range(4):
                                        kt = 8 * kp + 4 * half + k
                                        lo = (kp * 4 + k) * 2048 \
                                            + 2 * (ip * wT) + half
                                        mms.append(
                                            (planes_s[:, lo:lo + 2 * wT - 1:2],
                                             xnTo2_s[:, kt * TPC + tg * 512:
                                                     kt * TPC + tg * 512
                                                     + 512]))
                            if "nofold" not in dbg:
                                mms.append(
                                    (zfold_s[:, ip * wT:(ip + 1) * wT],
                                     xs_s[:, tg * 512:(tg + 1) * 512]))
                            fns = []
                            for q_, (lhsT, rhs) in enumerate(mms):
                                fns.append((lambda e, lhsT=lhsT, rhs=rhs,
                                            bank=bank, wT=wT,
                                            first=(q_ == 0),
                                            last=(q_ == len(mms) - 1):
                                            e.matmul(bank[0:wT, :], lhsT,
                                                     rhs, start=first,
                                                     stop=last)))
                            pe(fns, [planes_B, xnTo2_B, zfold_B, xs_B],
                               [bankB])
                            if "fence_ip" in dbg:
                                p.fence()
                            if region == 0:
                                act((lambda e, bank=bank, ip=ip, tg=tg,
                                     wT=wT: e.activation(
                                         out=gtmp_s[0:wT,
                                                    (ip * nTG + tg) * 512:
                                                    (ip * nTG + tg) * 512
                                                    + 512],
                                         in_=bank[0:wT, :],
                                         func=(AF.Copy if "nosilu" in dbg
                                               else AF.Silu))),
                                    [bankB], [gtmp_B])
                            else:
                                hst, hstB = hstage[tg]
                                dve((lambda e, bank=bank, ip=ip, tg=tg,
                                     wT=wT, hst=hst: e.tensor_tensor(
                                         out=hst[0:wT, ip * 512:
                                                 ip * 512 + 512],
                                         in0=gtmp_s[0:wT,
                                                    (ip * nTG + tg) * 512:
                                                    (ip * nTG + tg) * 512
                                                    + 512],
                                         in1=bank[0:wT, :], op=ALU.mult)),
                                    (gtmp_B, bankB), (hstB,))
                                if "nohs" not in dbg:
                                    pe([(lambda e, P=P, ip=ip, tg=tg, wT=wT,
                                         hst=hst: e.matmul(
                                             pb[6 + tg][0:nGd, :],
                                             b86_s[0:wT,
                                                   P * nGd:(P + 1) * nGd],
                                             hst[0:wT,
                                                 ip * 512:ip * 512 + 512],
                                             start=(P == 0 and ip == 0),
                                             stop=(P == nPd - 1
                                                   and ip == 7)))],
                                        [b86_B, hstB], [pbB[6 + tg]])
                    if "fence_reg" in dbg:
                        p.fence()
                    if region == 1:
                        for tg in range(nTG):
                            hst, hstB = hstage[tg]
                            dma("sync", (lambda e, P=P, tg=tg, hst=hst:
                                         e.dma_start(
                                             out=h_scr.rearrange(
                                                 "(j q) t -> q j t", q=128)[
                                                 :, 8 * P:8 * P + 8,
                                                 tg * 512:(tg + 1) * 512],
                                             in_=hst[:].rearrange(
                                                 "q (j t) -> q j t", j=8))),
                                (hstB,), (h_scr_B,))
            if "nohs" not in dbg:
                for tg in range(nTG):
                    act((lambda e, tg=tg: e.activation(
                        out=hs_s[:, tg * 512:(tg + 1) * 512],
                        in_=pb[6 + tg][0:nGd, :], func=AF.Copy)),
                        [pbB[6 + tg]], [hs_B])
            p.fence()

        if tapout("hsc", (8 * nPd * 128, TPC), F16):
            dma("sync", lambda e: e.dma_start(out=tap_aps["hsc"], in_=h_scr),
                (h_scr_B,), (out_B,))
        if tapout("wos", (TPC, H), F16):
            dma("sync", lambda e: e.dma_start(out=tap_aps["wos"], in_=wo_scr),
                (wo_scr_B,), (out_B,))

        s6 = ExitStack()
        with s6:
            scbd_s, scbd_B = sb(s6, "scbd", (128, nPd * 512), F16)
            scdw_s, scdw_B = sb(s6, "scdw", (nGd, 512), F32)
            scdwb_s, scdwb_B = sb(s6, "scdwb", (nGd, 512), F16)
            scdwn_s, scdwn_B = sb(s6, "scdwn", (nGd, 512), F16)
            qzdw_s, qzdw_B = sb(s6, "qzdw", (nGd, 64), U32)
            zq8dw_s, zq8dw_B = sb(s6, "zq8dw", (nGd, 512), U16)
            zcdw_s, zcdw_B = sb(s6, "zcdw", (nGd, 512), F16)
            dslab_s, dslab_B = sb(s6, "dslab", (128, 512), U32)
            duscr_s, duscr_B = sb(s6, "duscr", (128, 1024), U16)
            dplanes = [sb(s6, f"dplanes{i}", (128, 4 * 1024), F16)
                       for i in range(2)]
            hbuf = [sb(s6, f"hbuf{i}", (128, 8 * TPC), F16)
                    for i in range(2)]
            xsl_s, xsl_B = sb(s6, "xsl", (128, nJT * 512), F32)
            wsl_s, wsl_B = sb(s6, "wsl", (128, nJT * 512), F16)
            outst_s, outst_B = sb(s6, "outst", (128, nJT * 512), F32)
            nacc = nTG * 4

            for nw in range(H // 512):
                dma("sync", (lambda e, nw=nw: e.dma_start(
                    out=xsl_s[:].rearrange("q (tt c) -> q tt c", tt=nJT),
                    in_=x_own.rearrange("(tt q) c -> q tt c", q=128)[
                        :, :, nw * 512:(nw + 1) * 512])), (xsl_B,), (xsl_B,))
                dma("scalar", (lambda e, nw=nw: e.dma_start(
                    out=wsl_s[:].rearrange("q (tt c) -> q tt c", tt=nJT),
                    in_=wo_scr.rearrange("(tt q) c -> q tt c", q=128)[
                        :, :, nw * 512:(nw + 1) * 512])),
                    (wsl_B, wo_scr_B), (wsl_B,))
                dma("sync", (lambda e, nw=nw: e.dma_start(
                    out=scdw_s[:], in_=scd[:, nw * 512:(nw + 1) * 512])),
                    (scdw_B,), (scdw_B,))
                dve(lambda e: e.tensor_copy(out=scdwb_s[:], in_=scdw_s[:]),
                    (scdw_B,), (scdwb_B,))
                dma("sync", (lambda e, nw=nw: e.dma_start(
                    out=qzdw_s[:], in_=qzd[:, nw * 64:(nw + 1) * 64])),
                    (qzdw_B,), (qzdw_B,))
                for jz in range(8):
                    half, kq = jz // 4, jz % 4
                    dve((lambda e, jz=jz, half=half: e.tensor_scalar(
                        out=scdwn_s[:, jz * 64:(jz + 1) * 64],
                        in0=scdw_s[:, jz:512:8],
                        scalar1=-1.0, scalar2=None, op0=ALU.mult)),
                        (scdw_B,), (scdwn_B,))
                    if kq == 0:
                        fzq = (lambda e, jz=jz, half=half: e.tensor_scalar(
                            out=zq8dw_s[:, jz * 64:(jz + 1) * 64],
                            in0=qzdw_s[:].bitcast(U16)[:, half:half + 127:2],
                            scalar1=15, scalar2=None,
                            op0=ALU.bitwise_and))
                    elif kq == 3:
                        fzq = (lambda e, jz=jz, half=half: e.tensor_scalar(
                            out=zq8dw_s[:, jz * 64:(jz + 1) * 64],
                            in0=qzdw_s[:].bitcast(U16)[:, half:half + 127:2],
                            scalar1=12, scalar2=None,
                            op0=ALU.logical_shift_right))
                    else:
                        fzq = (lambda e, jz=jz, half=half, kq=kq:
                               e.tensor_scalar(
                                   out=zq8dw_s[:, jz * 64:(jz + 1) * 64],
                                   in0=qzdw_s[:].bitcast(U16)[
                                       :, half:half + 127:2],
                                   scalar1=4 * kq, scalar2=15,
                                   op0=ALU.logical_shift_right,
                                   op1=ALU.bitwise_and))
                    dve(fzq, (qzdw_B,), (zq8dw_B,))
                dve(lambda e: e.scalar_tensor_tensor(
                    out=zcdw_s[:], in0=zq8dw_s[:], scalar=1.0,
                    in1=scdwn_s[:], op0=ALU.add, op1=ALU.mult),
                    (zq8dw_B, scdwn_B), (zcdw_B,))
                for P in range(nPd):
                    w = _dwidth(d, P)
                    pe([(lambda e, P=P, nw=nw, w=w: e.matmul(
                        pb[6][0:w, :], ed_s[:, P * 128:P * 128 + w],
                        scdwb_s[:], start=True, stop=True))],
                        [ed_B, scdwb_B], [pbB[6]])
                    act((lambda e, P=P, w=w: e.activation(
                        out=scbd_s[0:w, P * 512:(P + 1) * 512],
                        in_=pb[6][0:w, :], func=AF.Copy)),
                        [pbB[6]], [scbd_B])
                for P in range(nPd):
                    w = _dwidth(d, P)
                    dpl, dplB = dplanes[P % 2]
                    dma("sync", (lambda e, P=P, nw=nw, w=w: e.dma_start(
                        out=dslab_s[0:w, :],
                        in_=qwd[P * 128:P * 128 + w,
                                nw * 512:(nw + 1) * 512])),
                        (dslab_B,), (dslab_B,))
                    for k in range(4):
                        if k == 0:
                            fn = (lambda e, w=w: e.tensor_scalar(
                                out=duscr_s[0:w, :],
                                in0=dslab_s[0:w, :].bitcast(U16),
                                scalar1=15, scalar2=None,
                                op0=ALU.bitwise_and))
                        elif k == 3:
                            fn = (lambda e, w=w: e.tensor_scalar(
                                out=duscr_s[0:w, :],
                                in0=dslab_s[0:w, :].bitcast(U16),
                                scalar1=12, scalar2=None,
                                op0=ALU.logical_shift_right))
                        else:
                            fn = (lambda e, w=w, k=k: e.tensor_scalar(
                                out=duscr_s[0:w, :],
                                in0=dslab_s[0:w, :].bitcast(U16),
                                scalar1=4 * k, scalar2=15,
                                op0=ALU.logical_shift_right,
                                op1=ALU.bitwise_and))
                        dve(fn, (dslab_B,), (duscr_B,))
                        dve((lambda e, k=k, w=w, P=P, dpl=dpl:
                             e.tensor_tensor(
                                 out=dpl[0:w, k * 1024:(k + 1) * 1024]
                                 .rearrange("q (cc two) -> q cc two", two=2),
                                 in0=duscr_s[0:w, :].rearrange(
                                     "q (cc two) -> q cc two", two=2),
                                 in1=scbd_s[0:w, P * 512:(P + 1) * 512]
                                 .unsqueeze(2).broadcast_to([w, 512, 2]),
                                 op=ALU.mult)),
                            (duscr_B, scbd_B), (dplB,))
                    hb, hbB = hbuf[P % 2]
                    dma("scalar", (lambda e, P=P, hb=hb: e.dma_start(
                        out=hb[:].rearrange("q (j t) -> q j t", j=8),
                        in_=h_scr.rearrange("(j q) t -> q j t", q=128)[
                            :, 8 * P:8 * P + 8, :])),
                        (hbB, h_scr_B), (hbB,))
                    fns = []
                    for j in range(8):
                        half, k = j // 4, j % 4
                        for tg in range(nTG):
                            for t in range(4):
                                bank = pb[(tg * 4 + t) % 8]
                                lhsT = hb[0:w, j * TPC + tg * 512 + t * 128:
                                          j * TPC + tg * 512 + t * 128 + 128]
                                rhs = dpl[0:w, k * 1024 + half:
                                          (k + 1) * 1024:2]
                                fns.append((lambda e, lhsT=lhsT, rhs=rhs,
                                            bank=bank,
                                            first=(P == 0 and j == 0):
                                            e.matmul(bank[:], lhsT, rhs,
                                                     start=first,
                                                     stop=False)))
                    pe(fns, [hbB, dplB], [pbB[i] for i in range(nacc)])
                fns = []
                for tg in range(nTG):
                    for t in range(4):
                        bank = pb[(tg * 4 + t) % 8]
                        lhsT = hs_s[:, tg * 512 + t * 128:
                                    tg * 512 + t * 128 + 128]
                        rhs = zcdw_s[:].rearrange(
                            "g (i a) -> g a i", i=8)
                        fns.append((lambda e, lhsT=lhsT, rhs=rhs, bank=bank:
                                    e.matmul(bank[:], lhsT, rhs,
                                             start=False, stop=True)))
                pe(fns, [hs_B, zcdw_B], [pbB[i] for i in range(nacc)])
                for tg in range(nTG):
                    for t in range(4):
                        tt = tg * 4 + t
                        bank, bankB = pb[tt % 8], pbB[tt % 8]
                        dve((lambda e, bank=bank, tt=tt: e.tensor_tensor(
                            out=outst_s[:, tt * 512:(tt + 1) * 512],
                            in0=bank[:],
                            in1=xsl_s[:, tt * 512:(tt + 1) * 512],
                            op=ALU.add)), (bankB, xsl_B), (outst_B,))
                        dve((lambda e, tt=tt: e.tensor_tensor(
                            out=outst_s[:, tt * 512:(tt + 1) * 512],
                            in0=outst_s[:, tt * 512:(tt + 1) * 512],
                            in1=wsl_s[:, tt * 512:(tt + 1) * 512],
                            op=ALU.add)), (outst_B, wsl_B), (outst_B,))
                dma("sync", (lambda e, nw=nw: e.dma_start(
                    out=out_own.rearrange("(tt q) c -> q tt c", q=128)[
                        :, :, nw * 512:(nw + 1) * 512],
                    in_=outst_s[:].rearrange("q (tt c) -> q tt c", tt=nJT))),
                    (outst_B,), (out_B,))

        p.fence()
        p.emit("sync", lambda e: e.nop(), reads=(out_B,), writes=())

        p.realize(es)
    return nc, p



def _module_weights(inputs):
    W = {}
    for k, v in inputs.items():
        if k.startswith("L0_attn_"):
            W[k[len("L0_attn_"):]] = np.asarray(v)
        elif k.startswith("L0_ffn_"):
            W[k[len("L0_ffn_"):]] = np.asarray(v)
    return W


def kernel(**inputs):
    c = cfg_full()
    d = _derived(c)
    W = _module_weights(inputs)
    shared = _shared_weight_views(d, W)
    consts = _consts(d)
    in_maps = [core_inputs(d, W, cid, shared, consts) for cid in range(8)]
    nc, _ = build_core_program(c)
    res = run_bass_kernel_spmd(nc, in_maps, core_ids=list(range(8)))
    out = np.empty((c["B"], c["S"], c["H"]), dtype=np.float32)
    for cid in range(8):
        b, par = divmod(cid, 2)
        qt = _qtiles_for_parity(d, par)
        rows = np.concatenate([np.arange(g * 128, (g + 1) * 128) for g in qt])
        out[b][rows] = res.results[cid]["out_own"]
    return out


# version 09 (final)
# speedup vs baseline: 1.0693x; 1.0693x over previous
import numpy as np
import ml_dtypes
from contextlib import ExitStack

import concourse.bass as bass
import concourse.mybir as mybir
from concourse.bass_utils import run_bass_kernel_spmd

F32 = mybir.dt.float32
BF16 = mybir.dt.bfloat16
U32 = mybir.dt.uint32
U16 = mybir.dt.uint16
U8 = mybir.dt.uint8
F16 = mybir.dt.float16
AF = mybir.ActivationFunctionType
ALU = mybir.AluOpType
AX = mybir.AxisListType.X
BF16NP = ml_dtypes.bfloat16

KG = 512
EPS_RMS = 1e-6
EPS_LN = 1e-5


def cfg_full():
    return dict(B=4, S=2048, H=4096, NH=32, DN=128, DR=64, DV=128,
                QL=1536, KL=512, FI=11008, HPG=4)


def cfg_small():
    return dict(B=4, S=1024, H=1024, NH=4, DN=128, DR=64, DV=128,
                QL=512, KL=512, FI=2048, HPG=4)


def _derived(c):
    d = dict(c)
    d["nKT"] = c["H"] // 128
    d["nQT"] = c["S"] // 128
    d["TPC"] = c["S"] // 2
    d["nJT"] = d["TPC"] // 128
    d["nKpack"] = c["H"] // 1024
    d["nG"] = c["H"] // 128
    d["nGd"] = c["FI"] // 128
    d["packd"] = c["FI"] // 8
    d["nPd"] = (d["packd"] + 127) // 128
    d["nQLT"] = c["QL"] // 128
    d["nKLT"] = c["KL"] // 128
    d["nKVa"] = (c["KL"] + c["DR"] + 127) // 128
    d["nHG"] = c["NH"] // c["HPG"]
    d["nKC"] = c["S"] // KG
    d["nTG"] = d["TPC"] // KG
    d["nNT"] = d["TPC"] // 512
    d["nKo"] = c["NH"] * c["DV"] // 128
    assert c["H"] % 1024 == 0 and d["TPC"] % 512 == 0
    return d


def _qtiles_for_parity(d, parity):
    picks = (0, 1, 6, 7) if parity == 0 else (2, 3, 4, 5)
    return [g for g in range(d["nQT"]) if g % 8 in picks]


def _counts(d, qt):
    return [g // 4 + 1 for g in qt]


def _dwidth(d, P):
    return min(128, d["packd"] - 128 * P)


def _gu_positions(d):
    pos, acc = [], 0
    for T in range(8 * d["nPd"]):
        pos.append(acc)
        acc += _dwidth(d, T // 8)
    pos.append(acc)
    assert acc == d["FI"]
    return pos


def _sigma_h(d):
    H = d["H"]
    sig = np.empty(H, dtype=np.int64)
    for kp in range(d["nKpack"]):
        for i in range(8):
            for p in range(128):
                sig[1024 * kp + 128 * i + p] = 1024 * kp + 8 * p + i
    return sig


def _gate_perm(d):
    pos = _gu_positions(d)
    perm = np.empty(d["FI"], dtype=np.int64)
    for T in range(8 * d["nPd"]):
        w = _dwidth(d, T // 8)
        P, i = T // 8, T % 8
        for cc in range(w):
            perm[pos[T] + cc] = 1024 * P + 8 * cc + i
    return perm


def _np_bf16(x):
    return np.ascontiguousarray(np.asarray(x, dtype=np.float32).astype(BF16NP))


def _np_f16(x):
    return np.ascontiguousarray(np.asarray(x, dtype=np.float32).astype(np.float16))


def _consts(d):
    out = {}
    out["identb"] = _np_bf16(np.eye(128))
    out["identh"] = _np_f16(np.eye(128))
    out["identf"] = np.ascontiguousarray(np.eye(128, dtype=np.float32))
    out["onescb"] = _np_f16(np.ones((128, 1)))
    out["ones1f"] = np.ones((1, 128), dtype=np.float32)
    sw = np.zeros((d["DR"], d["DR"]), dtype=np.float32)
    for k in range(d["DR"]):
        sw[k, k ^ 1] = 1.0
    out["swapb"] = _np_f16(sw)
    eg = np.zeros((d["nKpack"], d["nG"], 128), dtype=np.float32)
    for kp in range(d["nKpack"]):
        for p in range(128):
            eg[kp, 8 * kp + p // 16, p] = 1.0
    out["egb"] = _np_f16(eg.reshape(d["nKpack"] * d["nG"], 128))
    b32 = np.zeros((d["nKpack"], 128, d["nG"]), dtype=np.float32)
    for kp in range(d["nKpack"]):
        for p in range(128):
            b32[kp, p, 8 * kp + p // 16] = 1.0
    out["b32b"] = _np_f16(b32.reshape(d["nKpack"] * 128, d["nG"]))
    b86 = np.zeros((d["nPd"], 128, d["nGd"]), dtype=np.float32)
    for P in range(d["nPd"]):
        for p in range(_dwidth(d, P)):
            b86[P, p, 8 * P + p // 16] = 1.0
    out["b86b"] = _np_f16(b86.reshape(d["nPd"] * 128, d["nGd"]))
    ed = np.zeros((d["nPd"], d["nGd"], 128), dtype=np.float32)
    for P in range(d["nPd"]):
        for p in range(_dwidth(d, P)):
            ed[P, 8 * P + p // 16, p] = 1.0
    out["edb"] = _np_f16(ed.reshape(d["nPd"] * d["nGd"], 128))
    return out


def _rope_tables(d, positions):
    DR = d["DR"]
    freqs = 1.0 / 10000.0 ** (np.arange(0, DR, 2, dtype=np.float64) / DR)
    ang = np.outer(np.asarray(positions, dtype=np.float64), freqs)
    cos = np.cos(ang).astype(np.float32).T
    sin = np.sin(ang).astype(np.float32).T
    cose = np.repeat(cos, 2, axis=0)
    sine2 = np.empty_like(cose)
    sine2[0::2] = -sin
    sine2[1::2] = sin
    return np.ascontiguousarray(cose), np.ascontiguousarray(sine2)


def _shared_weight_views(d, W):
    sig = _sigma_h(d)
    perm = _gate_perm(d)
    FI = d["FI"]
    v = {}
    v["wqaT"] = np.ascontiguousarray(np.asarray(W["wq_a"]).T[sig])
    v["wkvaT"] = np.ascontiguousarray(np.asarray(W["wkv_a"]).T[sig])
    v["wqbT"] = np.ascontiguousarray(np.asarray(W["wq_b"]).T)
    v["wkvbT"] = np.ascontiguousarray(np.asarray(W["wkv_b"]).T)
    v["woT"] = np.ascontiguousarray(np.asarray(W["wo"]).T)
    qg = np.asarray(W["qweight_gu"])
    sg = np.asarray(W["scales_gu"])
    v["qwg_g"] = np.ascontiguousarray(qg[:, perm])
    v["qwg_u"] = np.ascontiguousarray(qg[:, FI + perm])
    v["scg_g"] = np.ascontiguousarray(sg[:, perm])
    v["scg_u"] = np.ascontiguousarray(sg[:, FI + perm])
    v["scg_nat"] = np.ascontiguousarray(sg)
    v["qzg"] = np.ascontiguousarray(np.asarray(W["qzeros_gu"]))
    v["qwd"] = np.ascontiguousarray(np.asarray(W["qweight_down"]))
    v["scd"] = np.ascontiguousarray(np.asarray(W["scales_down"]))
    v["qzd"] = np.ascontiguousarray(np.asarray(W["qzeros_down"]))
    v["qnw"] = np.ascontiguousarray(
        np.asarray(W["q_norm_w"]).reshape(d["nQLT"], 128).T)
    v["kvnw"] = np.ascontiguousarray(
        np.asarray(W["kv_norm_w"]).reshape(d["nKLT"], 128).T)
    return v


def core_inputs(d, W, core_id, shared=None, consts=None):
    if shared is None:
        shared = _shared_weight_views(d, W)
    if consts is None:
        consts = _consts(d)
    b, par = core_id // 2, core_id % 2
    qt = _qtiles_for_parity(d, par)
    cnt = _counts(d, qt)
    own_rows = np.concatenate([np.arange(g * 128, (g + 1) * 128) for g in qt])
    x_all = np.ascontiguousarray(np.asarray(W["x"])[b], dtype=np.float32)
    x_own = np.ascontiguousarray(x_all[own_rows])
    masks = np.zeros((d["nJT"], 128, KG), dtype=BF16NP)
    for j, g in enumerate(qt):
        delta = 128 * g - KG * (cnt[j] - 1)
        f = np.arange(KG)[None, :]
        pp = np.arange(128)[:, None]
        masks[j] = (f <= pp + delta).astype(BF16NP)
    cose_k, sine2_k = _rope_tables(d, np.arange(d["S"]))
    cose_q, sine2_q = _rope_tables(d, own_rows)
    m = dict(x_all=x_all, x_own=x_own, masks=np.ascontiguousarray(masks),
             cose_k=cose_k, sine2_k=sine2_k,
             cose_q=cose_q.astype(np.float16),
             sine2_q=sine2_q.astype(np.float16))
    m.update(shared)
    m.update(consts)
    return m



ENGS = ("tensor", "vector", "scalar", "gpsimd", "sync")


class Buf:
    __slots__ = ("name", "w", "r", "dsem", "dcount", "dqueue")

    def __init__(self, name=""):
        self.name = name
        self.w = {}
        self.r = {}
        self.dsem = None
        self.dcount = 0
        self.dqueue = None


class _Op:
    __slots__ = ("fns", "waits", "inc_sem", "inc_amt", "tag")

    def __init__(self, fns, waits, inc_sem, inc_amt, tag=""):
        self.fns = fns
        self.waits = waits
        self.inc_sem = inc_sem
        self.inc_amt = inc_amt
        self.tag = tag


class Prog:
    def __init__(self, nc):
        self.nc = nc
        self.ops = {e: [] for e in ENGS}
        self.count = {e: 0 for e in ENGS}
        self.water = {e: {} for e in ENGS}
        self.pending = {e: {} for e in ENGS}
        self.dsems = {}

    @staticmethod
    def _merge(dst, src):
        for k, v in src.items():
            if dst.get(k, -1) < v:
                dst[k] = v

    def emit(self, eng, fn=None, reads=(), writes=(), dma=False, fns=None,
             tag=""):
        import inspect
        if not tag:
            fr = inspect.currentframe().f_back
            if fr.f_code.co_name in ("pe", "act", "dve", "dma"):
                fr = fr.f_back
            tag = f"{fr.f_code.co_name}:{fr.f_lineno}"
        want = {}
        self._merge(want, self.pending[eng])
        self.pending[eng] = {}
        for bb in reads:
            self._merge(want, bb.w)
        for bb in writes:
            self._merge(want, bb.w)
            self._merge(want, bb.r)
        waits = []
        wm = self.water[eng]
        for sname, val in sorted(want.items()):
            if eng == "tensor" and sname == "tensor":
                continue
            if wm.get(sname, -1) >= val:
                continue
            wm[sname] = val
            waits.append((sname, val))
        if dma:
            assert writes, "dma op must declare a written buf"
            tb = writes[0]
            if tb.dsem is None:
                tb.dsem = "ds_" + tb.name
                tb.dqueue = eng
                assert tb.dsem not in self.dsems, tb.dsem
                self.dsems[tb.dsem] = 0
            assert tb.dqueue == eng, \
                f"buf {tb.name} dma'd from both {tb.dqueue} and {eng}"
            self.dsems[tb.dsem] += 16
            tok = {tb.dsem: self.dsems[tb.dsem]}
            self.ops[eng].append(_Op(fns if fns is not None else [fn], waits,
                                     tb.dsem, 16, tag))
            for bb in reads:
                self._merge(bb.r, tok)
            for bb in writes:
                bb.w = dict(tok)
                bb.r = {}
            return tok
        amt = 1
        self.count[eng] += amt
        tok = {eng: self.count[eng]}
        self.ops[eng].append(_Op(fns if fns is not None else [fn], waits,
                                 eng, amt, tag))
        for bb in reads:
            self._merge(bb.r, tok)
        for bb in writes:
            bb.w = dict(tok)
            bb.r = {}
        return tok

    def fence(self):
        snap = {e: cnt for e, cnt in self.count.items() if cnt > 0}
        snap.update({s: cnt for s, cnt in self.dsems.items() if cnt > 0})
        for e in ENGS:
            self._merge(self.pending[e], snap)

    def n_instructions(self):
        return sum(len(v) + sum(len(o.fns) - 1 for o in v)
                   for v in self.ops.values())

    def realize(self, es):
        handles = {e: es.enter_context(self.nc.semaphore("sem_" + e))
                   for e in ENGS}
        for s in self.dsems:
            handles[s] = es.enter_context(self.nc.semaphore(s))
        block = es.enter_context(self.nc.Block())
        prog = self

        def mk(ename):
            def body(eng):
                for op in prog.ops[ename]:
                    for sname, val in op.waits:
                        eng.wait_ge(handles[sname], val)
                    inst = None
                    for f in op.fns:
                        inst = f(eng)
                    if inst is not None:
                        inst.then_inc(handles[op.inc_sem], op.inc_amt)
            return body

        block.tensor(mk("tensor"))
        block.vector(mk("vector"))
        block.scalar(mk("scalar"))
        block.gpsimd(mk("gpsimd"))
        block.sync(mk("sync"))



def build_core_program(c, taps=(), dbg=()):
    d = _derived(c)
    H, S, NH, DN, DR, DV = c["H"], c["S"], c["NH"], c["DN"], c["DR"], c["DV"]
    QL, KL, FI, HPG = c["QL"], c["KL"], c["FI"], c["HPG"]
    nKT, nJT, TPC, nKC = d["nKT"], d["nJT"], d["TPC"], d["nKC"]
    nQLT, nKLT, nKVa = d["nQLT"], d["nKLT"], d["nKVa"]
    nKpack, nG, nGd, nPd = d["nKpack"], d["nG"], d["nGd"], d["nPd"]
    nHG, nNT, nTG, nKo = d["nHG"], d["nNT"], d["nTG"], d["nKo"]
    DH = DN + DR
    DKV = DN + DV
    cnt = _counts(d, _qtiles_for_parity(d, 0))
    scale = 1.0 / float(np.sqrt(DH))
    gupos = _gu_positions(d)

    nc = bass.Bass("TRN2", target_bir_lowering=False)

    def din(name, shape, dt):
        return nc.dram_tensor(name, list(shape), dt, kind="ExternalInput").ap()

    x_all = din("x_all", (S, H), F32)
    x_own = din("x_own", (TPC, H), F32)
    wqaT = din("wqaT", (H, QL), F32)
    wkvaT = din("wkvaT", (H, KL + DR), F32)
    wqbT = din("wqbT", (QL, NH * DH), F32)
    wkvbT = din("wkvbT", (KL, NH * DKV), F32)
    woT = din("woT", (NH * DV, H), F32)
    qwg_g = din("qwg_g", (H // 8, FI), U32)
    qwg_u = din("qwg_u", (H // 8, FI), U32)
    scg_g = din("scg_g", (nG, FI), F32)
    scg_u = din("scg_u", (nG, FI), F32)
    scg_nat = din("scg_nat", (nG, 2 * FI), F32)
    qzg = din("qzg", (nG, 2 * FI // 8), U32)
    qwd = din("qwd", (d["packd"], H), U32)
    scd = din("scd", (nGd, H), F32)
    qzd = din("qzd", (nGd, H // 8), U32)
    qnw = din("qnw", (128, nQLT), F32)
    kvnw = din("kvnw", (128, nKLT), F32)
    masks = din("masks", (nJT, 128, KG), BF16)
    cose_k = din("cose_k", (DR, S), F32)
    sine2_k = din("sine2_k", (DR, S), F32)
    cose_q = din("cose_q", (DR, TPC), F16)
    sine2_q = din("sine2_q", (DR, TPC), F16)
    identb = din("identb", (128, 128), BF16)
    identh = din("identh", (128, 128), F16)
    identf = din("identf", (128, 128), F32)
    onescb = din("onescb", (128, 1), F16)
    ones1f = din("ones1f", (1, 128), F32)
    swapb = din("swapb", (DR, DR), F16)
    egb = din("egb", (nKpack * nG, 128), F16)
    b32b = din("b32b", (nKpack * 128, nG), F16)
    b86b = din("b86b", (nPd * 128, nGd), F16)
    edb = din("edb", (nPd * nGd, 128), F16)

    out_own = nc.dram_tensor("out_own", [TPC, H], F32,
                             kind="ExternalOutput").ap()
    xnT_scr = nc.dram_tensor("xnT_scr", [H, TPC], F16, kind="Internal").ap()
    h_scr = nc.dram_tensor("h_scr", [8 * nPd * 128, TPC], F16,
                           kind="Internal").ap()
    wo_scr = nc.dram_tensor("wo_scr", [TPC, H], F16, kind="Internal").ap()
    attn_scr = nc.dram_tensor("attn_scr", [NH * 128, TPC], F16,
                              kind="Internal").ap()
    tap_aps = {}

    def tapout(name, shape, dt):
        if name in taps:
            tap_aps[name] = nc.dram_tensor("tap_" + name, list(shape), dt,
                                           kind="ExternalOutput").ap()
            return True
        return False

    p = Prog(nc)
    out_B = Buf("out")
    xnT_scr_B = Buf("xnT_scr")
    h_scr_B = Buf("h_scr")
    wo_scr_B = Buf("wo_scr")
    attn_scr_B = Buf("attn_scr")

    def sb(stack, name, shape, dt):
        t = stack.enter_context(nc.sbuf_tensor(name, list(shape), dt))
        return t, Buf(name)

    def pe(fns, reads, writes):
        return p.emit("tensor", None, fns=fns, reads=tuple(reads),
                      writes=tuple(writes))

    def act(fn, reads, writes):
        return p.emit("scalar", fn, reads=tuple(reads), writes=tuple(writes))

    def dve(fn, reads, writes):
        return p.emit("vector", fn, reads=tuple(reads), writes=tuple(writes))

    def dma(q, fn, reads, writes):
        return p.emit(q, fn, reads=tuple(reads), writes=tuple(writes), dma=True)

    es = ExitStack()
    with es:
        pb, pbB = [], []
        for i in range(8):
            t = es.enter_context(nc.psum_tensor(f"pb{i}", [128, 512], F32))
            pb.append(t)
            pbB.append(Buf(f"pb{i}"))

        identb_s, identb_B = sb(es, "identb_s", (128, 128), BF16)
        identh_s, identh_B = sb(es, "identh_s", (128, 128), F16)
        identf_s, identf_B = sb(es, "identf_s", (128, 128), F32)
        onescb_s, onescb_B = sb(es, "onescb_s", (128, 1), F16)
        ones1f_s, ones1f_B = sb(es, "ones1f_s", (1, 128), F32)
        swap_s, swap_B = sb(es, "swap_s", (DR, DR), F16)
        qnw_s, qnw_B = sb(es, "qnw_s", (128, nQLT), F32)
        kvnw_s, kvnw_B = sb(es, "kvnw_s", (128, nKLT), F32)
        eg_s, eg_B = sb(es, "eg_s", (nG, nKpack * 128), F16)
        b32_s, b32_B = sb(es, "b32_s", (128, nKpack * nG), F16)
        b86_s, b86_B = sb(es, "b86_s", (128, nPd * nGd), F16)
        ed_s, ed_B = sb(es, "ed_s", (nGd, nPd * 128), F16)
        hs_s, hs_B = sb(es, "hs", (nGd, nTG * 512), F16)
        xs_s, xs_B = sb(es, "xs", (nG, nTG * 512), F16)

        for t_, bb_, src_ in (
                (identb_s, identb_B, identb), (identh_s, identh_B, identh),
                (identf_s, identf_B, identf),
                (onescb_s, onescb_B, onescb), (ones1f_s, ones1f_B, ones1f),
                (swap_s, swap_B, swapb), (qnw_s, qnw_B, qnw),
                (kvnw_s, kvnw_B, kvnw)):
            dma("sync", (lambda e, t_=t_, src_=src_: e.dma_start(
                out=t_[:], in_=src_)), (), (bb_,))
        dma("sync", lambda e: e.dma_start(
            out=eg_s[:].rearrange("g (kp q) -> g kp q", kp=nKpack),
            in_=egb.rearrange("(kp g) q -> g kp q", g=nG)), (), (eg_B,))
        dma("sync", lambda e: e.dma_start(
            out=b32_s[:].rearrange("q (kp g) -> q kp g", kp=nKpack),
            in_=b32b.rearrange("(kp q) g -> q kp g", q=128)), (), (b32_B,))
        dma("sync", lambda e: e.dma_start(
            out=b86_s[:].rearrange("q (pp g) -> q pp g", pp=nPd),
            in_=b86b.rearrange("(pp q) g -> q pp g", q=128)), (), (b86_B,))
        dma("sync", lambda e: e.dma_start(
            out=ed_s[:].rearrange("g (pp q) -> g pp q", pp=nPd),
            in_=edb.rearrange("(pp g) q -> g pp q", g=nGd)), (), (ed_B,))

        def emit_ln(src_ap, xb, xbB, xnb, xnbB, stat_s, stat_B):
            dma("sync", (lambda e, src_ap=src_ap, xb=xb:
                         e.dma_start(out=xb[:], in_=src_ap)), (), (xbB,))
            dve((lambda e, xb=xb: e.tensor_reduce(
                stat_s[:, 0:1], xb[:], AX, ALU.add)), (xbB,), (stat_B,))
            act((lambda e, xb=xb, xnb=xnb: e.activation(
                out=xnb[:].bitcast(F16), in_=xb[:], func=AF.Square,
                accum_out=stat_s[:, 1:2])), (xbB,), (xnbB, stat_B))
            dve(lambda e: e.tensor_scalar(
                out=stat_s[:, 2:3], in0=stat_s[:, 0:1], scalar1=1.0 / H,
                scalar2=None, op0=ALU.mult), (stat_B,), (stat_B,))
            dve(lambda e: e.tensor_tensor(
                out=stat_s[:, 3:4], in0=stat_s[:, 2:3], in1=stat_s[:, 2:3],
                op=ALU.mult), (stat_B,), (stat_B,))
            dve(lambda e: e.tensor_scalar(
                out=stat_s[:, 4:5], in0=stat_s[:, 1:2], scalar1=1.0 / H,
                scalar2=EPS_LN, op0=ALU.mult, op1=ALU.add), (stat_B,), (stat_B,))
            dve(lambda e: e.tensor_tensor(
                out=stat_s[:, 4:5], in0=stat_s[:, 4:5], in1=stat_s[:, 3:4],
                op=ALU.subtract), (stat_B,), (stat_B,))
            act(lambda e: e.activation(
                out=stat_s[:, 5:6], in_=stat_s[:, 4:5], func=AF.Ln),
                (stat_B,), (stat_B,))
            act(lambda e: e.activation(
                out=stat_s[:, 5:6], in_=stat_s[:, 5:6], func=AF.Exp,
                scale=-0.5), (stat_B,), (stat_B,))
            dve(lambda e: e.scalar_tensor_tensor(
                out=stat_s[:, 6:7], in0=stat_s[:, 2:3], scalar=-1.0,
                in1=stat_s[:, 5:6], op0=ALU.mult, op1=ALU.mult),
                (stat_B,), (stat_B,))
            dve((lambda e, xb=xb, xnb=xnb: e.scalar_tensor_tensor(
                out=xnb[:], in0=xb[:], scalar=stat_s[:, 5:6],
                in1=stat_s[:, 6:7].broadcast_to([128, H]),
                op0=ALU.mult, op1=ALU.add)),
                (xbB, stat_B), (xnbB,))

        def emit_xpose(xnb, xnbB, dest_fn, dest_B):
            for a in range(nKT // 8):
                bank, bankB = pb[4 + (a % 2)], pbB[4 + (a % 2)]
                fns = []
                for i8 in range(8):
                    kt = 8 * a + i8
                    kp, i = kt // 8, kt % 8
                    src = xnb[:, 1024 * kp + i: 1024 * (kp + 1): 8]
                    dst = bank[:].bitcast(F16)[:, i8 * 128:(i8 + 1) * 128]
                    fns.append((lambda e, src=src, dst=dst:
                                e.transpose(dst, src, identh_s[:])))
                pe(fns, [xnbB, identh_B], [bankB])
                act((lambda e, bank=bank, dst_ap=dest_fn(a):
                     e.activation(out=dst_ap,
                                  in_=bank[:].bitcast(F16)[:, 0:1024]
                                  .rearrange("q (j f) -> q j f", j=8),
                                  func=AF.Copy)), [bankB], [dest_B])

        s1 = ExitStack()
        with s1:
            kvcn_s, kvcn_B = sb(s1, "kvcn", (128, nKLT * S), F16)
            kpe_s, kpe_B = sb(s1, "kpe", (DR, S), F16)
            kpesq_s, kpesq_B = sb(s1, "kpesq", (1, S), F16)
            masks_s, masks_B = sb(s1, "masks_s", (128, nJT * KG), BF16)
            sqbuf_s, sqbuf_B = sb(s1, "sqbuf", (128, 512), F16)
            stat2 = [sb(s1, f"stat{i}", (128, 8), F32) for i in range(2)]
            coseq_s, coseq_B = sb(s1, "coseq_s", (DR, TPC), F16)
            sineq_s, sineq_B = sb(s1, "sineq_s", (DR, TPC), F16)
            dma("sync", lambda e: e.dma_start(
                out=masks_s[:].rearrange("q (j f) -> q j f", j=nJT),
                in_=masks.rearrange("j q f -> q j f")), (), (masks_B,))
            dma("sync", lambda e: e.dma_start(out=coseq_s[:], in_=cose_q),
                (), (coseq_B,))
            dma("sync", lambda e: e.dma_start(out=sineq_s[:], in_=sine2_q),
                (), (sineq_B,))

            s2a = ExitStack()
            if True:
                with s2a:
                    cosek_s, cosek_B = sb(s2a, "cosek_s", (DR, S), F32)
                    sinek_s, sinek_B = sb(s2a, "sinek_s", (DR, S), F32)
                    dma("sync", lambda e: e.dma_start(out=cosek_s[:],
                                                      in_=cose_k),
                        (), (cosek_B,))
                    dma("sync", lambda e: e.dma_start(out=sinek_s[:],
                                                      in_=sine2_k),
                        (), (sinek_B,))
                    xbufs = [sb(s2a, f"xbuf{i}", (128, H), F32)
                             for i in range(1)]
                    xnbufs = [sb(s2a, f"xnbuf{i}", (128, H), F16)
                              for i in range(2)]
                    xstg_s, xstg_B = sb(s2a, "xstg", (128, nKT * 128), F16)

                    xsgv = xstg_s[:].rearrange("q (kt t) -> q kt t", kt=nKT)
                    for j in range(nJT):
                        xb, xbB = xbufs[0]
                        xnb, xnbB = xnbufs[j % 2]
                        st_, stB_ = stat2[j % 2]
                        emit_ln(x_own[j * 128:(j + 1) * 128, :],
                                xb, xbB, xnb, xnbB, st_, stB_)
                        emit_xpose(xnb, xnbB,
                                   (lambda a: xsgv[:, 8 * a: 8 * a + 8, :]),
                                   xstg_B)
                        dma("sync", (lambda e, j=j: e.dma_start(
                            out=xnT_scr.rearrange("(kt q) t -> q kt t",
                                                  q=128)[
                                :, :, j * 128:(j + 1) * 128],
                            in_=xsgv)), (xstg_B,), (xnT_scr_B,))

                    wkva_s, wkva_B = sb(s2a, "wkva", (128, nKT * (KL + DR)),
                                        F16)
                    dma("gpsimd", lambda e: e.dma_start(
                        out=wkva_s[:].rearrange("q (kt m) -> q kt m", kt=nKT),
                        in_=wkvaT.rearrange("(kt q) m -> q kt m", q=128)),
                        (), (wkva_B,))
                    xnTch_s, xnTch_B = sb(s2a, "xnTch", (128, nKT * KG), F16)
                    kvstat_s, kvstat_B = sb(s2a, "kvstat", (1, KG), F32)
                    kvrb_s, kvrb_B = sb(s2a, "kvrb", (128, KG), F32)
                    kpet_s, kpet_B = sb(s2a, "kpet", (DR, 3 * KG), F32)
                    kpebf_s, kpebf_B = sb(s2a, "kpebf", (DR, KG), F16)
                    xchv = xnTch_s[:].rearrange("q (kt t) -> q kt t", kt=nKT)

                    for ch in range(nKC):
                        t0 = ch * KG
                        for tt in range(KG // 128):
                            xb, xbB = xbufs[0]
                            xnb, xnbB = xnbufs[tt % 2]
                            st_, stB_ = stat2[tt % 2]
                            emit_ln(x_all[t0 + tt * 128:
                                          t0 + (tt + 1) * 128, :],
                                    xb, xbB, xnb, xnbB, st_, stB_)
                            emit_xpose(xnb, xnbB,
                                       (lambda a, tt=tt: xchv[
                                           :, 8 * a: 8 * a + 8,
                                           tt * 128:(tt + 1) * 128]),
                                       xnTch_B)
                        for mt in range(nKVa):
                            mw = min(128, KL + DR - mt * 128)
                            bank, bankB = pb[mt % 2], pbB[mt % 2]
                            fns = []
                            for kt in range(nKT):
                                lhsT = wkva_s[:, kt * (KL + DR) + mt * 128:
                                              kt * (KL + DR) + mt * 128 + mw]
                                rhs = xnTch_s[:, kt * KG:(kt + 1) * KG]
                                fns.append((lambda e, lhsT=lhsT, rhs=rhs,
                                            bank=bank, mw=mw, first=(kt == 0),
                                            last=(kt == nKT - 1):
                                            e.matmul(bank[0:mw, :], lhsT, rhs,
                                                     start=first, stop=last)))
                            pe(fns, [wkva_B, xnTch_B], [bankB])
                            if mt < nKLT:
                                act((lambda e, bank=bank, mt=mt, t0=t0:
                                     e.activation(
                                         out=kvcn_s[:, mt * S + t0:
                                                    mt * S + t0 + KG],
                                         in_=bank[:], func=AF.Copy)),
                                    [bankB], [kvcn_B])
                                act((lambda e, bank=bank: e.activation(
                                    out=sqbuf_s[:, 0:KG], in_=bank[:],
                                    func=AF.Square)), [bankB], [sqbuf_B])
                                pe([(lambda e, mt=mt: e.matmul(
                                    pb[2][0:1, :], onescb_s[:],
                                    sqbuf_s[:, 0:KG],
                                    start=(mt == 0), stop=(mt == nKLT - 1)))],
                                    [sqbuf_B, onescb_B], [pbB[2]])
                            else:
                                act((lambda e, bank=bank, mw=mw: e.activation(
                                    out=kpet_s[0:mw, 0:KG], in_=bank[0:mw, :],
                                    func=AF.Copy)), [bankB], [kpet_B])
                        dve(lambda e: e.tensor_scalar(
                            out=kvstat_s[:], in0=pb[2][0:1, :],
                            scalar1=1.0 / KL, scalar2=EPS_RMS,
                            op0=ALU.mult, op1=ALU.add), (pbB[2],), (kvstat_B,))
                        act(lambda e: e.activation(
                            out=kvstat_s[:], in_=kvstat_s[:], func=AF.Ln),
                            (kvstat_B,), (kvstat_B,))
                        act(lambda e: e.activation(
                            out=kvstat_s[:], in_=kvstat_s[:], func=AF.Exp,
                            scale=-0.5), (kvstat_B,), (kvstat_B,))
                        pe([lambda e: e.matmul(pb[3][:], ones1f_s[:],
                                               kvstat_s[:], start=True,
                                               stop=True)],
                           [ones1f_B, kvstat_B], [pbB[3]])
                        dve(lambda e: e.tensor_copy(out=kvrb_s[:],
                                                    in_=pb[3][:]),
                            (pbB[3],), (kvrb_B,))
                        for mt in range(nKLT):
                            dve((lambda e, mt=mt, t0=t0:
                                 e.scalar_tensor_tensor(
                                     out=kvcn_s[:, mt * S + t0:
                                                mt * S + t0 + KG],
                                     in0=kvcn_s[:, mt * S + t0:
                                                mt * S + t0 + KG],
                                     scalar=kvnw_s[:, mt: mt + 1],
                                     in1=kvrb_s[:],
                                     op0=ALU.mult, op1=ALU.mult)),
                                (kvcn_B, kvnw_B, kvrb_B), (kvcn_B,))
                        dve(lambda e: e.tensor_copy(out=kpebf_s[:],
                                                    in_=kpet_s[:, 0:KG]),
                            (kpet_B,), (kpebf_B,))
                        pe([lambda e: e.matmul(pb[3][0:DR, :], swap_s[:],
                                               kpebf_s[:], start=True,
                                               stop=True)],
                           [swap_B, kpebf_B], [pbB[3]])
                        act(lambda e: e.activation(
                            out=kpet_s[:, KG:2 * KG], in_=pb[3][0:DR, :],
                            func=AF.Copy), (pbB[3],), (kpet_B,))
                        dve((lambda e, t0=t0: e.tensor_tensor(
                            out=kpet_s[:, 2 * KG:3 * KG], in0=kpet_s[:, 0:KG],
                            in1=cosek_s[:, t0:t0 + KG], op=ALU.mult)),
                            (kpet_B, cosek_B), (kpet_B,))
                        dve((lambda e, t0=t0: e.tensor_tensor(
                            out=kpet_s[:, KG:2 * KG],
                            in0=kpet_s[:, KG:2 * KG],
                            in1=sinek_s[:, t0:t0 + KG], op=ALU.mult)),
                            (kpet_B, sinek_B), (kpet_B,))
                        dve((lambda e, t0=t0: e.tensor_tensor(
                            out=kpe_s[:, t0:t0 + KG],
                            in0=kpet_s[:, 2 * KG:3 * KG],
                            in1=kpet_s[:, KG:2 * KG], op=ALU.add)),
                            (kpet_B,), (kpe_B,))
                        dve((lambda e, t0=t0: e.tensor_tensor(
                            out=kpebf_s[:], in0=kpe_s[:, t0:t0 + KG],
                            in1=kpe_s[:, t0:t0 + KG], op=ALU.mult)),
                            (kpe_B,), (kpebf_B,))
                        pe([lambda e: e.matmul(pb[3][0:1, :],
                                               onescb_s[0:DR, :], kpebf_s[:],
                                               start=True, stop=True)],
                           [onescb_B, kpebf_B], [pbB[3]])
                        act((lambda e, t0=t0: e.activation(
                            out=kpesq_s[:, t0:t0 + KG], in_=pb[3][0:1, :],
                            func=AF.Copy)), (pbB[3],), (kpesq_B,))
                    p.fence()

                qan_s, qan_B = sb(s1, "qan", (128, nQLT * TPC), F16)
                s2b = ExitStack()
                with s2b:
                    xnTown_s, xnTown_B = sb(s2b, "xnTown",
                                            (128, nKT * TPC), F16)
                    dma("sync", lambda e: e.dma_start(
                        out=xnTown_s[:].rearrange("q (kt t) -> q kt t",
                                                  kt=nKT),
                        in_=xnT_scr.rearrange("(kt q) t -> q kt t", q=128)),
                        (xnT_scr_B,), (xnTown_B,))
                    wqa_s, wqa_B = sb(s2b, "wqa", (128, nKT * 256), F16)
                    qstat_s, qstat_B = sb(s2b, "qstat", (1, TPC), F32)
                    qrb_s, qrb_B = sb(s2b, "qrb", (128, TPC), F32)
                    for mp in range(nQLT // 2):
                        dma("gpsimd", (lambda e, mp=mp: e.dma_start(
                            out=wqa_s[:].rearrange("q (kt m) -> q kt m",
                                                   kt=nKT),
                            in_=wqaT.rearrange("(kt q) m -> q kt m", q=128)[
                                :, :, mp * 256:(mp + 1) * 256])),
                            (wqa_B,), (wqa_B,))
                        for mi in range(2):
                            mt = 2 * mp + mi
                            for nt in range(nNT):
                                bank = pb[(2 * mi + nt) % 4]
                                bankB = pbB[(2 * mi + nt) % 4]
                                fns = []
                                for kt in range(nKT):
                                    lhsT = wqa_s[:, kt * 256 + mi * 128:
                                                 kt * 256 + mi * 128 + 128]
                                    rhs = xnTown_s[:, kt * TPC + nt * 512:
                                                   kt * TPC + nt * 512 + 512]
                                    fns.append((lambda e, lhsT=lhsT, rhs=rhs,
                                                bank=bank, first=(kt == 0),
                                                last=(kt == nKT - 1):
                                                e.matmul(bank[:], lhsT, rhs,
                                                         start=first,
                                                         stop=last)))
                                pe(fns, [wqa_B, xnTown_B], [bankB])
                                act((lambda e, bank=bank, mt=mt, nt=nt:
                                     e.activation(
                                         out=qan_s[:, mt * TPC + nt * 512:
                                                   mt * TPC + nt * 512 + 512],
                                         in_=bank[:], func=AF.Copy)),
                                    [bankB], [qan_B])
                                act((lambda e, bank=bank: e.activation(
                                    out=sqbuf_s[:, 0:512], in_=bank[:],
                                    func=AF.Square)), [bankB], [sqbuf_B])
                                pe([(lambda e, mt=mt, nt=nt: e.matmul(
                                    pb[6 + nt][0:1, :], onescb_s[:],
                                    sqbuf_s[:, 0:512],
                                    start=(mt == 0),
                                    stop=(mt == nQLT - 1)))],
                                    [sqbuf_B, onescb_B], [pbB[6 + nt]])
                    for nt in range(nNT):
                        dve((lambda e, nt=nt: e.tensor_scalar(
                            out=qstat_s[:, nt * 512:(nt + 1) * 512],
                            in0=pb[6 + nt][0:1, :], scalar1=1.0 / QL,
                            scalar2=EPS_RMS, op0=ALU.mult, op1=ALU.add)),
                            (pbB[6 + nt],), (qstat_B,))
                    act(lambda e: e.activation(out=qstat_s[:], in_=qstat_s[:],
                                               func=AF.Ln),
                        (qstat_B,), (qstat_B,))
                    act(lambda e: e.activation(out=qstat_s[:], in_=qstat_s[:],
                                               func=AF.Exp, scale=-0.5),
                        (qstat_B,), (qstat_B,))
                    for nt in range(nNT):
                        pe([(lambda e, nt=nt: e.matmul(
                            pb[4][:], ones1f_s[:],
                            qstat_s[:, nt * 512:(nt + 1) * 512],
                            start=True, stop=True))],
                           [ones1f_B, qstat_B], [pbB[4]])
                        dve((lambda e, nt=nt: e.tensor_copy(
                            out=qrb_s[:, nt * 512:(nt + 1) * 512],
                            in_=pb[4][:])), (pbB[4],), (qrb_B,))
                    for mt in range(nQLT):
                        dve((lambda e, mt=mt: e.scalar_tensor_tensor(
                            out=qan_s[:, mt * TPC:(mt + 1) * TPC],
                            in0=qan_s[:, mt * TPC:(mt + 1) * TPC],
                            scalar=qnw_s[:, mt:mt + 1], in1=qrb_s[:],
                            op0=ALU.mult, op1=ALU.mult)),
                            (qan_B, qnw_B, qrb_B), (qan_B,))
                p.fence()

            if tapout("qan", (128, nQLT * TPC), F16):
                dma("sync", lambda e: e.dma_start(out=tap_aps["qan"],
                                                  in_=qan_s[:]),
                    (qan_B,), (out_B,))
            if tapout("kvcn", (128, nKLT * S), F16):
                dma("sync", lambda e: e.dma_start(out=tap_aps["kvcn"],
                                                  in_=kvcn_s[:]),
                    (kvcn_B,), (out_B,))
            if tapout("kpe", (DR, S), F16):
                dma("sync", lambda e: e.dma_start(out=tap_aps["kpe"],
                                                  in_=kpe_s[:]),
                    (kpe_B,), (out_B,))

            s3 = ExitStack()
            with s3:
                wqb2 = [sb(s3, f"wqb{i}", (128, nQLT * HPG * DH), F16)
                        for i in range(2)]
                wkvb2 = [sb(s3, f"wkvb{i}", (128, nKLT * HPG * DKV), F16)
                         for i in range(2)]
                qTn2 = [sb(s3, f"qTn{i}", (128, TPC), F16) for i in range(2)]
                qTr2 = [sb(s3, f"qTr{i}", (DR, TPC), F16) for i in range(2)]
                qrt_s, qrt_B = sb(s3, "qrt", (DR, 2 * 512), F16)
                qn_s, qn_B = sb(s3, "qn", (1, TPC), F32)
                nmrow2 = [sb(s3, f"nmrow{i}", (1, TPC), F32)
                          for i in range(2)]
                kn2 = [sb(s3, f"kn{i}", (128, S), F16) for i in range(2)]
                k2_s, k2_B = sb(s3, "k2", (1, S), F32)
                kmx_s, kmx_B = sb(s3, "kmx", (1, 2), F32)
                v42 = [sb(s3, f"v4{i}", (128, (S // 128) * HPG * 128),
                          BF16) for i in range(2)]
                atth2 = [sb(s3, f"atth{i}", (128, TPC), F16)
                         for i in range(2)]
                probs = [sb(s3, f"probs{i}", (128, KG), BF16)
                         for i in range(3)]
                probsT = [sb(s3, f"probsT{i}", (128, 512), BF16)
                          for i in range(2)]
                sums2 = [sb(s3, f"sums{i}", (128, 8), F32)
                         for i in range(nJT)]
                negm2 = [sb(s3, f"negm{i}", (128, nJT), F32)
                         for i in range(2)]
                inv_s, inv_B = sb(s3, "inv", (128, 1), F32)
                invr_s, invr_B = sb(s3, "invr", (1, 128), F32)
                invb_s, invb_B = sb(s3, "invb", (128, 128), F32)

                heads = [(g_, h4_) for g_ in range(nHG)
                         for h4_ in range(HPG)]
                pb4hB = [pbB[4], Buf("pb4h1")]
                pqctr = [0]

                def pbq():
                    pqctr[0] ^= 1
                    return pb[6 + pqctr[0]], pbB[6 + pqctr[0]]

                def group_dmas(g):
                    wqb_t, wqb_B = wqb2[g % 2]
                    wkvb_t, wkvb_B = wkvb2[g % 2]
                    dma("gpsimd", (lambda e, g=g, wqb_t=wqb_t: e.dma_start(
                        out=wqb_t[:].rearrange("q (kt m) -> q kt m", kt=nQLT),
                        in_=wqbT.rearrange("(kt q) m -> q kt m", q=128)[
                            :, :, g * HPG * DH:(g + 1) * HPG * DH])),
                        (wqb_B,), (wqb_B,))
                    dma("gpsimd", (lambda e, g=g, wkvb_t=wkvb_t: e.dma_start(
                        out=wkvb_t[:].rearrange("q (kt m) -> q kt m",
                                                kt=nKLT),
                        in_=wkvbT.rearrange("(kt q) m -> q kt m", q=128)[
                            :, :, g * HPG * DKV:(g + 1) * HPG * DKV])),
                        (wkvb_B,), (wkvb_B,))

                def prelude_gen(idx):
                    g, h4 = heads[idx]
                    h = g * HPG + h4
                    par_h = h % 2
                    qTn_s, qTn_B = qTn2[par_h]
                    qTr_s, qTr_B = qTr2[par_h]
                    nmrow_s, nmrow_B = nmrow2[par_h]
                    kn_s, kn_B = kn2[par_h]
                    wqb_t, wqb_B = wqb2[g % 2]
                    wkvb_t, wkvb_B = wkvb2[g % 2]
                    wqbv = wqb_t[:].rearrange("q (kt h m) -> q kt h m",
                                              kt=nQLT, h=HPG)
                    wkvbv = wkvb_t[:].rearrange("q (kt h m) -> q kt h m",
                                                kt=nKLT, h=HPG)
                    if h4 == 2 and g + 1 < nHG:
                        group_dmas(g + 1)
                    pend = []

                    def flush():
                        for c in pend:
                            c()
                        pend.clear()
                    if h4 == 0:
                        v4_s, v4_B = v42[g % 2]
                        for kt in range(S // 128):
                            bank, bankB = pbq()
                            fns = []
                            for klt in range(nKLT):
                                lhsT = kvcn_s[:, klt * S + kt * 128:
                                              klt * S + kt * 128 + 128]
                                rhs = wkvbv[:, klt, :, DN:DN + DV]
                                fns.append((lambda e, lhsT=lhsT, rhs=rhs,
                                            bank=bank, first=(klt == 0),
                                            last=(klt == nKLT - 1):
                                            e.matmul(bank[:], lhsT, rhs,
                                                     start=first,
                                                     stop=last)))
                            pe(fns, [wkvb_B, kvcn_B], [bankB])
                            act((lambda e, bank=bank, kt=kt, v4_s=v4_s:
                                 e.activation(
                                     out=v4_s[:, kt * HPG * 128:
                                              (kt + 1) * HPG * 128],
                                     in_=bank[:, 0:HPG * 128],
                                     func=AF.Copy)), [bankB], [v4_B])
                            if kt % 2 == 1:
                                yield
                    for sub in range(2):
                        moff = sub * 128
                        mw = 128 if sub == 0 else DH - 128
                        for nt in range(nNT):
                            bank, bankB = pbq()
                            flush()
                            fns = []
                            for kt in range(nQLT):
                                lhsT = wqbv[:, kt, h4, moff:moff + mw]
                                rhs = qan_s[:, kt * TPC + nt * 512:
                                            kt * TPC + nt * 512 + 512]
                                fns.append((lambda e, lhsT=lhsT, rhs=rhs,
                                            bank=bank, mw=mw,
                                            first=(kt == 0),
                                            last=(kt == nQLT - 1):
                                            e.matmul(bank[0:mw, :], lhsT,
                                                     rhs, start=first,
                                                     stop=last)))
                            pe(fns, [wqb_B, qan_B], [bankB])
                            if sub == 0:
                                act((lambda e, bank=bank, nt=nt,
                                     qTn_s=qTn_s:
                                     e.activation(
                                         out=qTn_s[:, nt * 512:
                                                   nt * 512 + 512],
                                         in_=bank[:], func=AF.Copy,
                                         scale=scale)), [bankB], [qTn_B])
                            else:
                                act((lambda e, bank=bank, nt=nt, mw=mw,
                                     qTr_s=qTr_s:
                                     e.activation(
                                         out=qTr_s[:, nt * 512:
                                                   nt * 512 + 512],
                                         in_=bank[0:mw, :], func=AF.Copy,
                                         scale=scale)), [bankB], [qTr_B])
                            act((lambda e, bank=bank, mw=mw:
                                 e.activation(
                                     out=sqbuf_s[0:mw, 0:512],
                                     in_=bank[0:mw, :], func=AF.Square)),
                                [bankB], [sqbuf_B])
                            def _qrow(sub=sub, nt=nt, mw=mw):
                                pe([(lambda e, mw=mw: e.matmul(
                                    pb[3][0:1, :], onescb_s[0:mw, :],
                                    sqbuf_s[0:mw, 0:512], start=True,
                                    stop=True))],
                                    [sqbuf_B, onescb_B], [pbB[3]])
                                if sub == 0:
                                    act((lambda e, nt=nt: e.activation(
                                        out=qn_s[:, nt * 512:
                                                 nt * 512 + 512],
                                        in_=pb[3][0:1, :], func=AF.Copy)),
                                        (pbB[3],), (qn_B,))
                                else:
                                    dve((lambda e, nt=nt: e.tensor_tensor(
                                        out=qn_s[:, nt * 512:
                                                 nt * 512 + 512],
                                        in0=pb[3][0:1, :],
                                        in1=qn_s[:, nt * 512:
                                                 nt * 512 + 512],
                                        op=ALU.add)), (pbB[3], qn_B),
                                        (qn_B,))
                            pend.append(_qrow)
                            yield
                    for nt in range(nNT):
                        t0q = nt * 512
                        bank, bankB = pbq()
                        flush()
                        pe([(lambda e, t0q=t0q, bank=bank, qTr_s=qTr_s:
                             e.matmul(
                                 bank[0:DR, :], swap_s[:],
                                 qTr_s[:, t0q:t0q + 512], start=True,
                                 stop=True))], [swap_B, qTr_B], [bankB])
                        act((lambda e, bank=bank: e.activation(
                            out=qrt_s[:, 512:1024], in_=bank[0:DR, :],
                            func=AF.Copy)), (bankB,), (qrt_B,))
                        dve((lambda e, t0q=t0q, qTr_s=qTr_s:
                             e.tensor_tensor(
                                 out=qrt_s[:, 0:512],
                                 in0=qTr_s[:, t0q:t0q + 512],
                                 in1=coseq_s[:, t0q:t0q + 512],
                                 op=ALU.mult)), (qTr_B, coseq_B),
                            (qrt_B,))
                        dve((lambda e, t0q=t0q: e.tensor_tensor(
                            out=qrt_s[:, 512:1024],
                            in0=qrt_s[:, 512:1024],
                            in1=sineq_s[:, t0q:t0q + 512],
                            op=ALU.mult)), (qrt_B, sineq_B), (qrt_B,))
                        dve((lambda e, t0q=t0q, qTr_s=qTr_s:
                             e.tensor_tensor(
                                 out=qTr_s[:, t0q:t0q + 512],
                                 in0=qrt_s[:, 0:512],
                                 in1=qrt_s[:, 512:1024],
                                 op=ALU.add)), (qrt_B,), (qTr_B,))
                    yield
                    act(lambda e: e.activation(
                        out=qn_s[:], in_=qn_s[:], func=AF.Ln),
                        (qn_B,), (qn_B,))
                    act(lambda e: e.activation(
                        out=qn_s[:], in_=qn_s[:], func=AF.Exp,
                        scale=0.5), (qn_B,), (qn_B,))
                    dve(lambda e: e.tensor_scalar(
                        out=qn_s[:], in0=qn_s[:], scalar1=scale,
                        scalar2=None, op0=ALU.mult), (qn_B,), (qn_B,))
                    for nt in range(S // 512):
                        bank, bankB = pbq()
                        flush()
                        fns = []
                        for klt in range(nKLT):
                            lhsT = wkvbv[:, klt, h4, 0:DN]
                            rhs = kvcn_s[:, klt * S + nt * 512:
                                         klt * S + nt * 512 + 512]
                            fns.append((lambda e, lhsT=lhsT, rhs=rhs,
                                        bank=bank, first=(klt == 0),
                                        last=(klt == nKLT - 1):
                                        e.matmul(bank[:], lhsT, rhs,
                                                 start=first, stop=last)))
                        pe(fns, [wkvb_B, kvcn_B], [bankB])
                        act((lambda e, bank=bank, nt=nt, kn_s=kn_s:
                             e.activation(
                                 out=kn_s[:, nt * 512:nt * 512 + 512],
                                 in_=bank[:], func=AF.Copy)),
                            [bankB], [kn_B])
                        act((lambda e, bank=bank: e.activation(
                            out=sqbuf_s[:, 0:512], in_=bank[:],
                            func=AF.Square)), [bankB], [sqbuf_B])

                        def _krow(nt=nt):
                            pe([lambda e: e.matmul(
                                pb[3][0:1, :], onescb_s[:],
                                sqbuf_s[:, 0:512], start=True, stop=True)],
                                [sqbuf_B, onescb_B], [pbB[3]])
                            dve((lambda e, nt=nt: e.tensor_tensor(
                                out=k2_s[:, nt * 512:(nt + 1) * 512],
                                in0=pb[3][0:1, :],
                                in1=kpesq_s[:, nt * 512:(nt + 1) * 512],
                                op=ALU.add)), (pbB[3], kpesq_B), (k2_B,))
                        pend.append(_krow)
                        yield
                    flush()
                    dve(lambda e: e.reduce_max(out=kmx_s[:, 0:1],
                                               in_=k2_s[:], axis=AX),
                        (k2_B,), (kmx_B,))
                    act(lambda e: e.activation(
                        out=kmx_s[:, 0:1], in_=kmx_s[:, 0:1], func=AF.Ln),
                        (kmx_B,), (kmx_B,))
                    act(lambda e: e.activation(
                        out=kmx_s[:, 0:1], in_=kmx_s[:, 0:1], func=AF.Exp,
                        scale=0.5), (kmx_B,), (kmx_B,))
                    dve((lambda e, nmrow_s=nmrow_s: e.tensor_scalar(
                        out=nmrow_s[:], in0=qn_s[:],
                        scalar1=kmx_s[:, 0:1], scalar2=-1.0,
                        op0=ALU.mult, op1=ALU.mult)),
                        (qn_B, kmx_B), (nmrow_B,))
                    yield
                    negm_s, negm_B = negm2[par_h]
                    fns = []
                    for j in range(nJT):
                        fns.append((lambda e, j=j, nmrow_s=nmrow_s:
                                    e.matmul(
                                        pb[3][0:128, j:j + 1],
                                        nmrow_s[:, j * 128:(j + 1) * 128],
                                        ones1f_s[:, 0:1], start=(j == 0),
                                        stop=(j == nJT - 1))))
                    pe(fns, [nmrow_B, ones1f_B], [pbB[3]])
                    act((lambda e, negm_s=negm_s: e.activation(
                        out=negm_s[:], in_=pb[3][0:128, 0:nJT],
                        func=AF.Copy)), (pbB[3],), (negm_B,))

                group_dmas(0)
                for _ in prelude_gen(0):
                    pass
                for idx in range(len(heads)):
                    g, h4 = heads[idx]
                    h = g * HPG + h4
                    par_h = h % 2
                    qTn_s, qTn_B = qTn2[par_h]
                    qTr_s, qTr_B = qTr2[par_h]
                    kn_s, kn_B = kn2[par_h]
                    negm_s, negm_B = negm2[par_h]
                    atth_s, atth_B = atth2[par_h]
                    v4_s, v4_B = v42[g % 2]
                    tasks = [(j, kg) for j in range(nJT)
                             for kg in range(cnt[j])]

                    def emit_scores(ti):
                        j, kg = tasks[ti]
                        bank, bankB = pb[ti % 2], pbB[ti % 2]
                        fns = [
                            (lambda e, j=j, kg=kg, bank=bank,
                             qTn_s=qTn_s, kn_s=kn_s:
                             e.matmul(bank[:],
                                      qTn_s[:, j * 128:j * 128 + 128],
                                      kn_s[:, kg * KG:(kg + 1) * KG],
                                      start=True, stop=False)),
                            (lambda e, j=j, kg=kg, bank=bank,
                             qTr_s=qTr_s:
                             e.matmul(bank[:],
                                      qTr_s[:, j * 128:j * 128 + 128],
                                      kpe_s[:, kg * KG:(kg + 1) * KG],
                                      start=False, stop=True)),
                        ]
                        pe(fns, [qTn_B, qTr_B, kn_B, kpe_B], [bankB])

                    srcs = {}

                    def emit_exp(ti):
                        j, kg = tasks[ti]
                        cj = cnt[j]
                        diag = (kg == cj - 1)
                        bank, bankB = pb[ti % 2], pbB[ti % 2]
                        sums_s, sums_B = sums2[j]
                        pr_t, pr_B = probs[ti % 3]
                        if not diag:
                            act((lambda e, bank=bank, pr_t=pr_t, kg=kg,
                                 j=j, negm_s=negm_s, sums_s=sums_s:
                                 e.activation(
                                     out=pr_t[:], in_=bank[:],
                                     func=AF.Exp,
                                     bias=negm_s[:, j:j + 1],
                                     accum_out=sums_s[:, kg:kg + 1])),
                                [bankB, negm_B], [pr_B, sums_B])
                            srcs[ti] = (pr_t, pr_B)
                        else:
                            act((lambda e, bank=bank, pr_t=pr_t, j=j,
                                 negm_s=negm_s: e.activation(
                                     out=pr_t[:], in_=bank[:],
                                     func=AF.Exp,
                                     bias=negm_s[:, j:j + 1])),
                                [bankB, negm_B], [pr_B])
                            dve((lambda e, pr_t=pr_t, j=j, kg=kg,
                                 sums_s=sums_s: e.scalar_tensor_tensor(
                                     out=pr_t[:], in0=pr_t[:],
                                     scalar=1.0,
                                     in1=masks_s[:, j * KG:(j + 1) * KG],
                                     op0=ALU.mult, op1=ALU.mult,
                                     accum_out=sums_s[:, kg:kg + 1])),
                                (pr_B, masks_B), (pr_B, sums_B))
                            srcs[ti] = (pr_t, pr_B)

                    def emit_transp(ti):
                        src_t, src_B = srcs.pop(ti)
                        half = (ti % 2) * 512
                        hB = pb4hB[ti % 2]
                        fns = []
                        for ks in range(4):
                            fns.append((lambda e, src_t=src_t, ks=ks,
                                        half=half:
                                        e.transpose(
                                            pb[4][:].bitcast(BF16)[
                                                :, half + ks * 128:
                                                half + (ks + 1) * 128],
                                            src_t[:, ks * 128:
                                                  (ks + 1) * 128],
                                            identb_s[:])))
                        pe(fns, [src_B, identb_B], [hB])
                        pT_t, pT_B = probsT[ti % 2]
                        dve((lambda e, pT_t=pT_t, half=half: e.tensor_copy(
                            out=pT_t[:],
                            in_=pb[4][:].bitcast(BF16)[:, half:half + 512])),
                            (hB,), (pT_B,))

                    tpend = []

                    def emit_pv(ti):
                        j, kg = tasks[ti]
                        cj = cnt[j]
                        diag = (kg == cj - 1)
                        pvb, pvbB = (pb[5], pbB[5]) if j % 2 == 0 else \
                            (pb[2], pbB[2])
                        sums_s, sums_B = sums2[j]
                        pT_t, pT_B = probsT[ti % 2]
                        fns = []
                        for ks in range(4):
                            fns.append(
                                (lambda e, kg=kg, ks=ks, h4=h4,
                                 pT_t=pT_t, cj=cj, pvb=pvb, v4_s=v4_s:
                                 e.matmul(
                                     pvb[:, 0:128],
                                     v4_s[:, (kg * 4 + ks) * HPG * 128
                                          + h4 * 128:
                                          (kg * 4 + ks) * HPG * 128
                                          + h4 * 128 + 128],
                                     pT_t[:, ks * 128:(ks + 1) * 128],
                                     start=(kg == 0 and ks == 0),
                                     stop=(kg == cj - 1 and ks == 3))))
                        pe(fns, [v4_B, pT_B], [pvbB])
                        if not diag:
                            return

                        def _diag(j=j, cj=cj, pvb=pvb, pvbB=pvbB,
                                  sums_s=sums_s, sums_B=sums_B,
                                  atth_s=atth_s, atth_B=atth_B):
                            dve((lambda e, cj=cj, sums_s=sums_s:
                                 e.tensor_reduce(
                                     out=inv_s[:], in_=sums_s[:, 0:cj],
                                     axis=AX, op=ALU.add)),
                                (sums_B,), (inv_B,))
                            dve(lambda e: e.tensor_scalar(
                                out=inv_s[:], in0=inv_s[:], scalar1=1e-30,
                                scalar2=None, op0=ALU.add),
                                (inv_B,), (inv_B,))
                            dve(lambda e: e.reciprocal(out=inv_s[:],
                                                       in_=inv_s[:]),
                                (inv_B,), (inv_B,))
                            pe([lambda e: e.matmul(
                                pb[3][0:1, 0:128], inv_s[:], identf_s[:],
                                start=True, stop=True)],
                               [inv_B, identf_B], [pbB[3]])
                            act(lambda e: e.activation(
                                out=invr_s[:], in_=pb[3][0:1, 0:128],
                                func=AF.Copy), (pbB[3],), (invr_B,))
                            pe([lambda e: e.matmul(
                                pb[3][:, 0:128], ones1f_s[:], invr_s[:],
                                start=True, stop=True)],
                               [invr_B, ones1f_B], [pbB[3]])
                            act(lambda e: e.activation(
                                out=invb_s[:], in_=pb[3][:, 0:128],
                                func=AF.Copy), (pbB[3],), (invb_B,))
                            dve((lambda e, j=j, pvb=pvb, atth_s=atth_s:
                                 e.tensor_tensor(
                                     out=atth_s[:, j * 128:
                                                j * 128 + 128],
                                     in0=pvb[:, 0:128], in1=invb_s[:],
                                     op=ALU.mult)),
                                (pvbB, invb_B), (atth_B,))
                        tpend.append(_diag)

                    ngen = (prelude_gen(idx + 1)
                            if idx + 1 < len(heads) else iter(()))
                    nT = len(tasks)
                    for ti in range(nT):
                        for c_ in tpend:
                            c_()
                        tpend.clear()
                        emit_scores(ti)
                        if ti > 0:
                            emit_exp(ti - 1)
                        if ti > 2:
                            emit_transp(ti - 3)
                        if ti > 3:
                            emit_pv(ti - 4)
                        next(ngen, None)
                    emit_exp(nT - 1)
                    if nT > 3:
                        for c_ in tpend:
                            c_()
                        tpend.clear()
                        emit_pv(nT - 4)
                    for tl in range(max(0, nT - 3), nT):
                        for c_ in tpend:
                            c_()
                        tpend.clear()
                        emit_transp(tl)
                        emit_pv(tl)
                    for c_ in tpend:
                        c_()
                    tpend.clear()
                    for _ in ngen:
                        pass
                    dma("sync", (lambda e, h=h, atth_s=atth_s:
                                 e.dma_start(
                                     out=attn_scr[h * 128:(h + 1) * 128,
                                                  :],
                                     in_=atth_s[:])),
                        (atth_B,), (attn_scr_B,))
                p.fence()

        if tapout("attn", (NH * 128, TPC), F16):
            dma("sync", lambda e: e.dma_start(out=tap_aps["attn"],
                                              in_=attn_scr),
                (attn_scr_B,), (out_B,))

        s4 = ExitStack()
        with s4:
            woT_b = [sb(s4, f"woTb{i}", (128, nKo * 512), F16)
                     for i in range(2)]
            acall_s, acall_B = sb(s4, "acall", (128, nJT * nKo * 128), F16)
            wost_s, wost_B = sb(s4, "wost", (128, nJT * 512), F16)
            for t in range(nJT):
                dma("sync", (lambda e, t=t: e.dma_start(
                    out=acall_s[:, t * nKo * 128:(t + 1) * nKo * 128]
                    .rearrange("q (kt m) -> q kt m", kt=nKo),
                    in_=attn_scr.rearrange("(kt q) m -> q kt m", q=128)[
                        :, :, t * 128:(t + 1) * 128])),
                    (acall_B, attn_scr_B), (acall_B,))
            for nw in range(H // 512):
                wt, wtB = woT_b[nw % 2]
                dma("gpsimd", (lambda e, nw=nw, wt=wt: e.dma_start(
                    out=wt[:].rearrange("q (kt m) -> q kt m", kt=nKo),
                    in_=woT.rearrange("(kt q) m -> q kt m", q=128)[
                        :, :, nw * 512:(nw + 1) * 512])), (wtB,), (wtB,))
                for t in range(nJT):
                    bank, bankB = pb[t % 2], pbB[t % 2]
                    fns = []
                    for kt in range(nKo):
                        lhsT = acall_s[:, (t * nKo + kt) * 128:
                                       (t * nKo + kt) * 128 + 128]
                        rhs = wt[:, kt * 512:(kt + 1) * 512]
                        fns.append((lambda e, lhsT=lhsT, rhs=rhs, bank=bank,
                                    first=(kt == 0), last=(kt == nKo - 1):
                                    e.matmul(bank[:], lhsT, rhs,
                                             start=first, stop=last)))
                    pe(fns, [acall_B, wtB], [bankB])
                    act((lambda e, bank=bank, t=t: e.activation(
                        out=wost_s[:, t * 512:(t + 1) * 512], in_=bank[:],
                        func=AF.Copy)), [bankB], [wost_B])
                dma("sync", (lambda e, nw=nw: e.dma_start(
                    out=wo_scr.rearrange("(tt q) c -> q tt c", q=128)[
                        :, :, nw * 512:(nw + 1) * 512],
                    in_=wost_s[:].rearrange("q (tt c) -> q tt c", tt=nJT))),
                    (wost_B,), (wo_scr_B,))
            p.fence()

        s5 = ExitStack()
        with s5:
            xnTo2_s, xnTo2_B = sb(s5, "xnTo2", (128, nKT * TPC), F16)
            dma("sync", lambda e: e.dma_start(
                out=xnTo2_s[:].rearrange("q (kt t) -> q kt t", kt=nKT),
                in_=xnT_scr.rearrange("(kt q) t -> q kt t", q=128)),
                (xnT_scr_B,), (xnTo2_B,))
            s5a = ExitStack()
            with s5a:
                for tg in range(nTG):
                    fns = []
                    for kt in range(nKT):
                        kp = kt // 8
                        fns.append((lambda e, kt=kt, kp=kp, tg=tg: e.matmul(
                            pb[2][0:nG, :], b32_s[:, kp * nG:(kp + 1) * nG],
                            xnTo2_s[:, kt * TPC + tg * 512:
                                    kt * TPC + tg * 512 + 512],
                            start=(kt == 0), stop=(kt == nKT - 1))))
                    pe(fns, [b32_B, xnTo2_B], [pbB[2]])
                    act((lambda e, tg=tg: e.activation(
                        out=xs_s[:, tg * 512:(tg + 1) * 512],
                        in_=pb[2][0:nG, :], func=AF.Copy)), [pbB[2]], [xs_B])
                p.fence()

            slab2 = [sb(s5, f"slab{i}", (128, 1024), U32) for i in range(2)]
            uscr_s, uscr_B = sb(s5, "uscr", (128, 2048), U16)
            planes_s, planes_B = sb(s5, "planes", (128, nKpack * 4 * 2048),
                                    F16)
            scgwb_s, scgwb_B = sb(s5, "scgwb", (nG, 1024), F16)
            scgwbn_s, scgwbn_B = sb(s5, "scgwbn", (nG, 1024), F16)
            qzgw_s, qzgw_B = sb(s5, "qzgw", (nG, 128), U32)
            zq8w_s, zq8w_B = sb(s5, "zq8w", (nG, 128), U16)
            zfold_s, zfold_B = sb(s5, "zfold", (nG, 8 * 128), F16)
            scb_s, scb_B = sb(s5, "scb", (128, nKpack * 2048), F16)
            gtmp_s, gtmp_B = sb(s5, "gtmp", (128, 8 * nTG * 512), F16)
            hstage = [sb(s5, f"hstage{i}", (128, 8 * 512), F16)
                      for i in range(nTG)]

            for P in range(nPd):
                win = gupos[8 * (P + 1)] - gupos[8 * P]
                wT = _dwidth(d, P)
                for region in range(2):
                    qsrc = qwg_g if region == 0 else qwg_u
                    ssrc = scg_g if region == 0 else scg_u
                    dma("gpsimd", (lambda e, P=P, ssrc=ssrc, win=win:
                                   e.dma_start(
                                       out=scgwb_s[:, 0:win],
                                       in_=ssrc[:, gupos[8 * P]:
                                                gupos[8 * P] + win])),
                        (scgwb_B,), (scgwb_B,))
                    dve((lambda e, win=win: e.tensor_scalar(
                        out=scgwbn_s[:, 0:win], in0=scgwb_s[:, 0:win],
                        scalar1=-1.0, scalar2=None, op0=ALU.mult)),
                        (scgwb_B,), (scgwbn_B,))
                    dma("sync", (lambda e, P=P, region=region, wT=wT:
                                 e.dma_start(
                                     out=qzgw_s[:, 0:wT],
                                     in_=qzg[:, region * (FI // 8) + 128 * P:
                                             region * (FI // 8) + 128 * P
                                             + wT])), (qzgw_B,), (qzgw_B,))
                    for ip in range(8):
                        half, kq = ip // 4, ip % 4
                        if kq == 0:
                            fzq = (lambda e, half=half, wT=wT:
                                   e.tensor_scalar(
                                       out=zq8w_s[:, 0:wT],
                                       in0=qzgw_s[:, 0:wT].bitcast(U16)[
                                           :, half:half + 2 * wT - 1:2],
                                       scalar1=15, scalar2=None,
                                       op0=ALU.bitwise_and))
                        elif kq == 3:
                            fzq = (lambda e, half=half, wT=wT:
                                   e.tensor_scalar(
                                       out=zq8w_s[:, 0:wT],
                                       in0=qzgw_s[:, 0:wT].bitcast(U16)[
                                           :, half:half + 2 * wT - 1:2],
                                       scalar1=12, scalar2=None,
                                       op0=ALU.logical_shift_right))
                        else:
                            fzq = (lambda e, half=half, kq=kq, wT=wT:
                                   e.tensor_scalar(
                                       out=zq8w_s[:, 0:wT],
                                       in0=qzgw_s[:, 0:wT].bitcast(U16)[
                                           :, half:half + 2 * wT - 1:2],
                                       scalar1=4 * kq, scalar2=15,
                                       op0=ALU.logical_shift_right,
                                       op1=ALU.bitwise_and))
                        dve(fzq, (qzgw_B,), (zq8w_B,))
                        dve((lambda e, ip=ip, wT=wT:
                             e.scalar_tensor_tensor(
                                 out=zfold_s[:, ip * wT:(ip + 1) * wT],
                                 in0=zq8w_s[:, 0:wT], scalar=1.0,
                                 in1=scgwbn_s[:, ip * wT:(ip + 1) * wT],
                                 op0=ALU.add, op1=ALU.mult)),
                            (zq8w_B, scgwbn_B), (zfold_B,))
                    for kp in range(nKpack):
                        for cs in range(0, win, 256):
                            cw = min(256, win - cs)
                            pe([(lambda e, kp=kp, cs=cs, cw=cw: e.matmul(
                                pb[4][:, 0:2 * cw],
                                eg_s[:, kp * 128:(kp + 1) * 128],
                                scgwb_s[:, cs:cs + cw].unsqueeze(2)
                                .broadcast_to([nG, cw, 2]), start=True,
                                stop=True))], [eg_B, scgwb_B], [pbB[4]])
                            act((lambda e, kp=kp, cs=cs, cw=cw: e.activation(
                                out=scb_s[:, kp * 2048 + 2 * cs:
                                          kp * 2048 + 2 * cs + 2 * cw],
                                in_=pb[4][:, 0:2 * cw], func=AF.Copy)),
                                [pbB[4]], [scb_B])
                        slab_s, slab_B = slab2[kp % 2]
                        dma("sync", (lambda e, P=P, kp=kp, qsrc=qsrc, win=win,
                                     slab_s=slab_s: e.dma_start(
                                         out=slab_s[:, 0:win],
                                         in_=qsrc[kp * 128:(kp + 1) * 128,
                                                  gupos[8 * P]:
                                                  gupos[8 * P] + win])),
                            (slab_B,), (slab_B,))
                        for k in range(4):
                            if k == 0:
                                fn = (lambda e, win=win, slab_s=slab_s:
                                      e.tensor_scalar(
                                          out=uscr_s[:, 0:2 * win],
                                          in0=slab_s[:, 0:win].bitcast(U16),
                                          scalar1=15, scalar2=None,
                                          op0=ALU.bitwise_and))
                            elif k == 3:
                                fn = (lambda e, win=win, slab_s=slab_s:
                                      e.tensor_scalar(
                                          out=uscr_s[:, 0:2 * win],
                                          in0=slab_s[:, 0:win].bitcast(U16),
                                          scalar1=12, scalar2=None,
                                          op0=ALU.logical_shift_right))
                            else:
                                fn = (lambda e, win=win, k=k, slab_s=slab_s:
                                      e.tensor_scalar(
                                          out=uscr_s[:, 0:2 * win],
                                          in0=slab_s[:, 0:win].bitcast(U16),
                                          scalar1=4 * k, scalar2=15,
                                          op0=ALU.logical_shift_right,
                                          op1=ALU.bitwise_and))
                            dve(fn, (slab_B,), (uscr_B,))
                            dve((lambda e, kp=kp, k=k, win=win:
                                 e.tensor_tensor(
                                     out=planes_s[:, (kp * 4 + k) * 2048:
                                                  (kp * 4 + k) * 2048
                                                  + 2 * win],
                                     in0=uscr_s[:, 0:2 * win],
                                     in1=scb_s[:, kp * 2048:
                                               kp * 2048 + 2 * win],
                                     op=ALU.mult)),
                                (uscr_B, scb_B), (planes_B,))
                    for ip in range(8):
                        for tg in range(nTG):
                            bank = pb[2 * (ip % 2) + (tg % 2)]
                            bankB = pbB[2 * (ip % 2) + (tg % 2)]
                            mms = []
                            for kp in range(nKpack):
                                for half in range(2):
                                    for k in range(4):
                                        kt = 8 * kp + 4 * half + k
                                        lo = (kp * 4 + k) * 2048 \
                                            + 2 * (ip * wT) + half
                                        mms.append(
                                            (planes_s[:, lo:lo + 2 * wT - 1:2],
                                             xnTo2_s[:, kt * TPC + tg * 512:
                                                     kt * TPC + tg * 512
                                                     + 512]))
                            if "nofold" not in dbg:
                                mms.append(
                                    (zfold_s[:, ip * wT:(ip + 1) * wT],
                                     xs_s[:, tg * 512:(tg + 1) * 512]))
                            fns = []
                            for q_, (lhsT, rhs) in enumerate(mms):
                                fns.append((lambda e, lhsT=lhsT, rhs=rhs,
                                            bank=bank, wT=wT,
                                            first=(q_ == 0),
                                            last=(q_ == len(mms) - 1):
                                            e.matmul(bank[0:wT, :], lhsT,
                                                     rhs, start=first,
                                                     stop=last)))
                            pe(fns, [planes_B, xnTo2_B, zfold_B, xs_B],
                               [bankB])
                            if "fence_ip" in dbg:
                                p.fence()
                            if region == 0:
                                act((lambda e, bank=bank, ip=ip, tg=tg,
                                     wT=wT: e.activation(
                                         out=gtmp_s[0:wT,
                                                    (ip * nTG + tg) * 512:
                                                    (ip * nTG + tg) * 512
                                                    + 512],
                                         in_=bank[0:wT, :],
                                         func=(AF.Copy if "nosilu" in dbg
                                               else AF.Silu))),
                                    [bankB], [gtmp_B])
                            else:
                                hst, hstB = hstage[tg]
                                dve((lambda e, bank=bank, ip=ip, tg=tg,
                                     wT=wT, hst=hst: e.tensor_tensor(
                                         out=hst[0:wT, ip * 512:
                                                 ip * 512 + 512],
                                         in0=gtmp_s[0:wT,
                                                    (ip * nTG + tg) * 512:
                                                    (ip * nTG + tg) * 512
                                                    + 512],
                                         in1=bank[0:wT, :], op=ALU.mult)),
                                    (gtmp_B, bankB), (hstB,))
                                if "nohs" not in dbg:
                                    pe([(lambda e, P=P, ip=ip, tg=tg, wT=wT,
                                         hst=hst: e.matmul(
                                             pb[6 + tg][0:nGd, :],
                                             b86_s[0:wT,
                                                   P * nGd:(P + 1) * nGd],
                                             hst[0:wT,
                                                 ip * 512:ip * 512 + 512],
                                             start=(P == 0 and ip == 0),
                                             stop=(P == nPd - 1
                                                   and ip == 7)))],
                                        [b86_B, hstB], [pbB[6 + tg]])
                    if "fence_reg" in dbg:
                        p.fence()
                    if region == 1:
                        for tg in range(nTG):
                            hst, hstB = hstage[tg]
                            dma("sync", (lambda e, P=P, tg=tg, hst=hst:
                                         e.dma_start(
                                             out=h_scr.rearrange(
                                                 "(j q) t -> q j t", q=128)[
                                                 :, 8 * P:8 * P + 8,
                                                 tg * 512:(tg + 1) * 512],
                                             in_=hst[:].rearrange(
                                                 "q (j t) -> q j t", j=8))),
                                (hstB,), (h_scr_B,))
            if "nohs" not in dbg:
                for tg in range(nTG):
                    act((lambda e, tg=tg: e.activation(
                        out=hs_s[:, tg * 512:(tg + 1) * 512],
                        in_=pb[6 + tg][0:nGd, :], func=AF.Copy)),
                        [pbB[6 + tg]], [hs_B])
            p.fence()

        if tapout("hsc", (8 * nPd * 128, TPC), F16):
            dma("sync", lambda e: e.dma_start(out=tap_aps["hsc"], in_=h_scr),
                (h_scr_B,), (out_B,))
        if tapout("wos", (TPC, H), F16):
            dma("sync", lambda e: e.dma_start(out=tap_aps["wos"], in_=wo_scr),
                (wo_scr_B,), (out_B,))

        s6 = ExitStack()
        with s6:
            scbd_s, scbd_B = sb(s6, "scbd", (128, nPd * 512), F16)
            scdw_s, scdw_B = sb(s6, "scdw", (nGd, 512), F32)
            scdwb_s, scdwb_B = sb(s6, "scdwb", (nGd, 512), F16)
            scdwn_s, scdwn_B = sb(s6, "scdwn", (nGd, 512), F16)
            qzdw_s, qzdw_B = sb(s6, "qzdw", (nGd, 64), U32)
            zq8dw_s, zq8dw_B = sb(s6, "zq8dw", (nGd, 512), U16)
            zcdw_s, zcdw_B = sb(s6, "zcdw", (nGd, 512), F16)
            dslab_s, dslab_B = sb(s6, "dslab", (128, 512), U32)
            duscr_s, duscr_B = sb(s6, "duscr", (128, 1024), U16)
            dplanes = [sb(s6, f"dplanes{i}", (128, 4 * 1024), F16)
                       for i in range(2)]
            hbuf = [sb(s6, f"hbuf{i}", (128, 8 * TPC), F16)
                    for i in range(2)]
            xsl_s, xsl_B = sb(s6, "xsl", (128, nJT * 512), F32)
            wsl_s, wsl_B = sb(s6, "wsl", (128, nJT * 512), F16)
            outst_s, outst_B = sb(s6, "outst", (128, nJT * 512), F32)
            nacc = nTG * 4

            for nw in range(H // 512):
                dma("sync", (lambda e, nw=nw: e.dma_start(
                    out=xsl_s[:].rearrange("q (tt c) -> q tt c", tt=nJT),
                    in_=x_own.rearrange("(tt q) c -> q tt c", q=128)[
                        :, :, nw * 512:(nw + 1) * 512])), (xsl_B,), (xsl_B,))
                dma("scalar", (lambda e, nw=nw: e.dma_start(
                    out=wsl_s[:].rearrange("q (tt c) -> q tt c", tt=nJT),
                    in_=wo_scr.rearrange("(tt q) c -> q tt c", q=128)[
                        :, :, nw * 512:(nw + 1) * 512])),
                    (wsl_B, wo_scr_B), (wsl_B,))
                dma("sync", (lambda e, nw=nw: e.dma_start(
                    out=scdw_s[:], in_=scd[:, nw * 512:(nw + 1) * 512])),
                    (scdw_B,), (scdw_B,))
                dve(lambda e: e.tensor_copy(out=scdwb_s[:], in_=scdw_s[:]),
                    (scdw_B,), (scdwb_B,))
                dma("sync", (lambda e, nw=nw: e.dma_start(
                    out=qzdw_s[:], in_=qzd[:, nw * 64:(nw + 1) * 64])),
                    (qzdw_B,), (qzdw_B,))
                for jz in range(8):
                    half, kq = jz // 4, jz % 4
                    dve((lambda e, jz=jz, half=half: e.tensor_scalar(
                        out=scdwn_s[:, jz * 64:(jz + 1) * 64],
                        in0=scdw_s[:, jz:512:8],
                        scalar1=-1.0, scalar2=None, op0=ALU.mult)),
                        (scdw_B,), (scdwn_B,))
                    if kq == 0:
                        fzq = (lambda e, jz=jz, half=half: e.tensor_scalar(
                            out=zq8dw_s[:, jz * 64:(jz + 1) * 64],
                            in0=qzdw_s[:].bitcast(U16)[:, half:half + 127:2],
                            scalar1=15, scalar2=None,
                            op0=ALU.bitwise_and))
                    elif kq == 3:
                        fzq = (lambda e, jz=jz, half=half: e.tensor_scalar(
                            out=zq8dw_s[:, jz * 64:(jz + 1) * 64],
                            in0=qzdw_s[:].bitcast(U16)[:, half:half + 127:2],
                            scalar1=12, scalar2=None,
                            op0=ALU.logical_shift_right))
                    else:
                        fzq = (lambda e, jz=jz, half=half, kq=kq:
                               e.tensor_scalar(
                                   out=zq8dw_s[:, jz * 64:(jz + 1) * 64],
                                   in0=qzdw_s[:].bitcast(U16)[
                                       :, half:half + 127:2],
                                   scalar1=4 * kq, scalar2=15,
                                   op0=ALU.logical_shift_right,
                                   op1=ALU.bitwise_and))
                    dve(fzq, (qzdw_B,), (zq8dw_B,))
                dve(lambda e: e.scalar_tensor_tensor(
                    out=zcdw_s[:], in0=zq8dw_s[:], scalar=1.0,
                    in1=scdwn_s[:], op0=ALU.add, op1=ALU.mult),
                    (zq8dw_B, scdwn_B), (zcdw_B,))
                for P in range(nPd):
                    w = _dwidth(d, P)
                    pe([(lambda e, P=P, nw=nw, w=w: e.matmul(
                        pb[6][0:w, :], ed_s[:, P * 128:P * 128 + w],
                        scdwb_s[:], start=True, stop=True))],
                        [ed_B, scdwb_B], [pbB[6]])
                    act((lambda e, P=P, w=w: e.activation(
                        out=scbd_s[0:w, P * 512:(P + 1) * 512],
                        in_=pb[6][0:w, :], func=AF.Copy)),
                        [pbB[6]], [scbd_B])
                for P in range(nPd):
                    w = _dwidth(d, P)
                    dpl, dplB = dplanes[P % 2]
                    dma("sync", (lambda e, P=P, nw=nw, w=w: e.dma_start(
                        out=dslab_s[0:w, :],
                        in_=qwd[P * 128:P * 128 + w,
                                nw * 512:(nw + 1) * 512])),
                        (dslab_B,), (dslab_B,))
                    for k in range(4):
                        if k == 0:
                            fn = (lambda e, w=w: e.tensor_scalar(
                                out=duscr_s[0:w, :],
                                in0=dslab_s[0:w, :].bitcast(U16),
                                scalar1=15, scalar2=None,
                                op0=ALU.bitwise_and))
                        elif k == 3:
                            fn = (lambda e, w=w: e.tensor_scalar(
                                out=duscr_s[0:w, :],
                                in0=dslab_s[0:w, :].bitcast(U16),
                                scalar1=12, scalar2=None,
                                op0=ALU.logical_shift_right))
                        else:
                            fn = (lambda e, w=w, k=k: e.tensor_scalar(
                                out=duscr_s[0:w, :],
                                in0=dslab_s[0:w, :].bitcast(U16),
                                scalar1=4 * k, scalar2=15,
                                op0=ALU.logical_shift_right,
                                op1=ALU.bitwise_and))
                        dve(fn, (dslab_B,), (duscr_B,))
                        dve((lambda e, k=k, w=w, P=P, dpl=dpl:
                             e.tensor_tensor(
                                 out=dpl[0:w, k * 1024:(k + 1) * 1024]
                                 .rearrange("q (cc two) -> q cc two", two=2),
                                 in0=duscr_s[0:w, :].rearrange(
                                     "q (cc two) -> q cc two", two=2),
                                 in1=scbd_s[0:w, P * 512:(P + 1) * 512]
                                 .unsqueeze(2).broadcast_to([w, 512, 2]),
                                 op=ALU.mult)),
                            (duscr_B, scbd_B), (dplB,))
                    hb, hbB = hbuf[P % 2]
                    dma("scalar", (lambda e, P=P, hb=hb: e.dma_start(
                        out=hb[:].rearrange("q (j t) -> q j t", j=8),
                        in_=h_scr.rearrange("(j q) t -> q j t", q=128)[
                            :, 8 * P:8 * P + 8, :])),
                        (hbB, h_scr_B), (hbB,))
                    fns = []
                    for j in range(8):
                        half, k = j // 4, j % 4
                        for tg in range(nTG):
                            for t in range(4):
                                bank = pb[(tg * 4 + t) % 8]
                                lhsT = hb[0:w, j * TPC + tg * 512 + t * 128:
                                          j * TPC + tg * 512 + t * 128 + 128]
                                rhs = dpl[0:w, k * 1024 + half:
                                          (k + 1) * 1024:2]
                                fns.append((lambda e, lhsT=lhsT, rhs=rhs,
                                            bank=bank,
                                            first=(P == 0 and j == 0):
                                            e.matmul(bank[:], lhsT, rhs,
                                                     start=first,
                                                     stop=False)))
                    pe(fns, [hbB, dplB], [pbB[i] for i in range(nacc)])
                fns = []
                for tg in range(nTG):
                    for t in range(4):
                        bank = pb[(tg * 4 + t) % 8]
                        lhsT = hs_s[:, tg * 512 + t * 128:
                                    tg * 512 + t * 128 + 128]
                        rhs = zcdw_s[:].rearrange(
                            "g (i a) -> g a i", i=8)
                        fns.append((lambda e, lhsT=lhsT, rhs=rhs, bank=bank:
                                    e.matmul(bank[:], lhsT, rhs,
                                             start=False, stop=True)))
                pe(fns, [hs_B, zcdw_B], [pbB[i] for i in range(nacc)])
                for tg in range(nTG):
                    for t in range(4):
                        tt = tg * 4 + t
                        bank, bankB = pb[tt % 8], pbB[tt % 8]
                        dve((lambda e, bank=bank, tt=tt: e.tensor_tensor(
                            out=outst_s[:, tt * 512:(tt + 1) * 512],
                            in0=bank[:],
                            in1=xsl_s[:, tt * 512:(tt + 1) * 512],
                            op=ALU.add)), (bankB, xsl_B), (outst_B,))
                        dve((lambda e, tt=tt: e.tensor_tensor(
                            out=outst_s[:, tt * 512:(tt + 1) * 512],
                            in0=outst_s[:, tt * 512:(tt + 1) * 512],
                            in1=wsl_s[:, tt * 512:(tt + 1) * 512],
                            op=ALU.add)), (outst_B, wsl_B), (outst_B,))
                dma("sync", (lambda e, nw=nw: e.dma_start(
                    out=out_own.rearrange("(tt q) c -> q tt c", q=128)[
                        :, :, nw * 512:(nw + 1) * 512],
                    in_=outst_s[:].rearrange("q (tt c) -> q tt c", tt=nJT))),
                    (outst_B,), (out_B,))

        p.fence()
        p.emit("sync", lambda e: e.nop(), reads=(out_B,), writes=())

        p.realize(es)
    return nc, p



def _module_weights(inputs):
    W = {}
    for k, v in inputs.items():
        if k.startswith("L0_attn_"):
            W[k[len("L0_attn_"):]] = np.asarray(v)
        elif k.startswith("L0_ffn_"):
            W[k[len("L0_ffn_"):]] = np.asarray(v)
    return W


def kernel(**inputs):
    c = cfg_full()
    d = _derived(c)
    W = _module_weights(inputs)
    shared = _shared_weight_views(d, W)
    consts = _consts(d)
    in_maps = [core_inputs(d, W, cid, shared, consts) for cid in range(8)]
    nc, _ = build_core_program(c)
    res = run_bass_kernel_spmd(nc, in_maps, core_ids=list(range(8)))
    out = np.empty((c["B"], c["S"], c["H"]), dtype=np.float32)
    for cid in range(8):
        b, par = divmod(cid, 2)
        qt = _qtiles_for_parity(d, par)
        rows = np.concatenate([np.arange(g * 128, (g + 1) * 128) for g in qt])
        out[b][rows] = res.results[cid]["out_own"]
    return out
